# version 08
# baseline (speedup 1.0000x reference)
import numpy as np
from contextlib import ExitStack

import concourse.bass as bass
import concourse.mybir as mybir
from concourse.bass_utils import run_bass_kernel_spmd
from concourse.bass_isa import ReduceOp

F32 = mybir.dt.float32
BF16 = mybir.dt.bfloat16
I16 = mybir.dt.int16
I8 = mybir.dt.int8
FP8 = mybir.dt.float8e4
AF = mybir.ActivationFunctionType
ALU = mybir.AluOpType
AX = mybir.AxisListType

L = 2048
DM = 1024
DI = 2048
DS = 128
NH = 32
HD = 64
DP = 4384
KC = 4
TC = 256
NT1 = L // TC
SUB = 128
INNER = 4096
NT2 = L // SUB
EPS_LN = 1e-5
EPS_RMS = 1e-6
N_CHUNK_W = 35
MAGIC = 12582912.0


class Chan:
    def __init__(self, stack, nc, name, dma=False, owner=None):
        self.sem = stack.enter_context(nc.semaphore(name))
        self.count = 0
        self.step = 16 if dma else 1
        self.name = name
        self.owner = owner


EMIT_BUDGET = None
DROP_SELF_WAITS = False
AUTO_SELF_WAIT = True
SERIAL_FROM = None


class Plan:
    ENGINES = ("sync", "scalar", "vector", "tensor", "gpsimd")

    def __init__(self):
        self.ops = {e: [] for e in self.ENGINES}
        self.total = 0
        self.all_chans = None
        self.eng_chan = None
        self.pending_wb = {}
        self._waited = {e: {} for e in self.ENGINES}
        self._fence = {e: [] for e in self.ENGINES}

    def fence(self, chans):
        for e in self.ENGINES:
            self._fence[e] = [(c, c.count) for c in chans]

    def emit(self, eng, fn, waits=(), inc=None, wb=False):
        if EMIT_BUDGET is not None and self.total >= EMIT_BUDGET:
            return
        if SERIAL_FROM is not None and self.total >= SERIAL_FROM \
                and self.all_chans is not None:
            waits = list(waits) + [(c, c.count) for c in self.all_chans]
        if AUTO_SELF_WAIT and self.eng_chan is not None:
            ch = self.eng_chan.get(eng)
            if ch is not None:
                waits = list(waits) + [(ch, ch.count)]
        if eng == "tensor" and self.eng_chan is not None:
            waits = list(waits) + [(self.eng_chan["scalar"],
                                    self.eng_chan["scalar"].count),
                                   (self.eng_chan["vector"],
                                    self.eng_chan["vector"].count)]
        elif self.pending_wb.get(eng) is not None:
            waits = list(waits) + [self.pending_wb[eng]]
            self.pending_wb[eng] = None
        self.total += 1
        wl = []
        pend = self._fence[eng]
        if pend:
            waits = list(pend) + list(waits)
            self._fence[eng] = []
        for w in waits:
            if isinstance(w, tuple):
                ch, val = w
            else:
                ch, val = w, w.count
            if val <= 0:
                continue
            if DROP_SELF_WAITS and ch.owner == eng:
                continue
            prev = self._waited[eng].get(ch.name, 0)
            if val > prev:
                wl.append((ch, val))
                self._waited[eng][ch.name] = val
        if inc is not None:
            inc.count += inc.step
        if wb and inc is not None:
            self.pending_wb[eng] = (inc, inc.count)
        self.ops[eng].append((fn, wl, inc))

    def barrier_waits(self, chans):
        return [(c, c.count) for c in chans]

    def replay(self, block, engines):
        for name in self.ENGINES:
            ops = self.ops[name]

            def body(e, ops=ops):
                for fn, wl, inc in ops:
                    for ch, val in wl:
                        e.wait_ge(ch.sem, val)
                    instr = fn(e)
                    if inc is not None:
                        if instr is None:
                            e.sem_inc(inc.sem, inc.step)
                        else:
                            instr.then_inc(inc.sem, inc.step)
            getattr(block, name)(body)


def build_program(n_p1_chunks=NT1, do_p2=True, dumps=(), x1s_out=False,
                  detect_races=False):
    nc = bass.Bass("TRN2", target_bir_lowering=False,
                   detect_race_conditions=detect_races)

    dram = {}

    def din(name, shape, dt=F32):
        dram[name] = nc.dram_tensor(name, shape, dt, kind="ExternalInput").ap()
        return dram[name]

    def dout(name, shape, dt=F32):
        dram[name] = nc.dram_tensor(name, shape, dt, kind="ExternalOutput").ap()
        return dram[name]

    def dint(name, shape, dt=F32):
        dram[name] = nc.dram_tensor(name, shape, dt, kind="Internal").ap()
        return dram[name]

    x_in = din("x_in", [L, DM])
    wiT = din("wiT", [DM, DP])
    woT = din("woT", [DI, DM])
    wgT = din("wgT", [DM, INNER])
    wuT = din("wuT", [DM, INNER])
    wdT = din("wdT", [INNER, DM])
    normw_i = din("normw", [128, 8])
    convw_i = din("convw", [128, 16 * KC])
    convb_i = din("convb", [128, 16])
    alog_i = din("alog", [32, 1])
    dtb_i = din("dtb", [32, 1])
    db_i = din("db", [128, DI])
    cid_i = din("c_id", [32, 32])
    cut_i = din("c_ut", [128, 128])
    crhsd_i = din("c_rhsd", [96, INNER])
    clhsd_i = din("c_lhsd", [96, 128])

    out = dout("out", [L, DM])
    x1s = (dout if x1s_out else dint)("x1s", [L, DM])
    wq_g = dint("wq_g", [DM, INNER], FP8)
    wq_u = dint("wq_u", [DM, INNER], FP8)
    wq_d = dint("wq_d", [INNER, DM], FP8)
    bscr = dint("bscr", [64])
    dscr = dint("dscr", [32])

    stack = ExitStack()
    sb = lambda name, shape, dt: stack.enter_context(nc.sbuf_tensor(name, shape, dt))
    ps = lambda name: stack.enter_context(nc.psum_tensor(name, [128, 512], F32))

    W = Chan(stack, nc, "chW", dma=True)
    T = Chan(stack, nc, "chT", dma=True)
    D = Chan(stack, nc, "chD", dma=True)
    C = Chan(stack, nc, "chC", dma=True)
    Q = Chan(stack, nc, "chQ", dma=True)
    U2 = Chan(stack, nc, "chU2", dma=True)
    U3 = Chan(stack, nc, "chU3", dma=True)
    TP = Chan(stack, nc, "chTP", dma=True)
    TX = Chan(stack, nc, "chTX", dma=True)
    TY = Chan(stack, nc, "chTY", dma=True)
    TB = Chan(stack, nc, "chTB", dma=True)
    TR = Chan(stack, nc, "chTR", dma=True)
    S = Chan(stack, nc, "chS", owner="scalar")
    V = Chan(stack, nc, "chV", owner="vector")
    E = Chan(stack, nc, "chE", owner="tensor")
    G = Chan(stack, nc, "chG", owner="gpsimd")
    ALL = (W, T, D, C, Q, U2, U3, TP, TX, TY, TB, TR, S, V, E, G)

    p = Plan()
    p.all_chans = ALL
    p.eng_chan = {"vector": V, "scalar": S, "gpsimd": G}

    mabs = sb("mabs", [128, 4], F32)
    scol = sb("scol", [128, 4], F32)
    racc = sb("racc", [128, 4], F32)
    rtmp = sb("rtmp", [128, 1], F32)
    p1stack = ExitStack()
    sb1 = lambda name, shape, dt: p1stack.enter_context(
        nc.sbuf_tensor(name, shape, dt))
    idf = sb1("idf", [32, 32], F32)
    utb = sb1("utb", [128, 128], BF16)
    normw = sb1("normw_t", [128, 8], F32)
    convw = sb1("convw_t", [128, 16 * KC], F32)
    convb = sb1("convb_t", [128, 16], F32)
    nega = sb1("nega", [32, 1], F32)
    dtb = sb1("dtb_t", [32, 1], F32)
    dbb = sb1("dbb", [128, DI], BF16)

    P = [ps(f"P{i}") for i in range(8)]

    def tiny(name, fn=None):
        return (fn or sb)(name, [128, 16], F32)

    def emit_rstd(win_eng_waits, var_col, out_col, eps):
        p.emit("vector", lambda e, a=var_col, b=out_col, ep=eps: e.tensor_scalar(
            b, a, ep, None, ALU.add), waits=win_eng_waits, inc=V)
        vv = (V, V.count)
        p.emit("scalar", lambda e, b=out_col: e.activation(
            b, b, AF.Ln), waits=[vv], inc=S)
        p.emit("scalar", lambda e, b=out_col: e.activation(
            b, b, AF.Exp, scale=-0.5), waits=(), inc=S)

    wiTb = sb1("wiTb", [128, 8 * DP], BF16)
    woTb = sb1("woTb", [128, 16 * DM], BF16)

    def dma(eng_ch, eng, dst, src):
        p.emit(eng, lambda e, d=dst, s=src: e.dma_start(out=d, in_=s),
               waits=(), inc=eng_ch)

    cidf_w = D
    p.emit("gpsimd", lambda e: e.dma_start(out=idf[:], in_=cid_i[:]), inc=D)
    p.emit("gpsimd", lambda e: e.dma_start(out=normw[:], in_=normw_i[:]), inc=D)
    p.emit("gpsimd", lambda e: e.dma_start(out=convw[:], in_=convw_i[:]), inc=D)
    p.emit("gpsimd", lambda e: e.dma_start(out=convb[:], in_=convb_i[:]), inc=D)
    p.emit("gpsimd", lambda e: e.dma_start(out=nega[:], in_=alog_i[:]), inc=D)
    p.emit("gpsimd", lambda e: e.dma_start(out=dtb[:], in_=dtb_i[:]), inc=D)
    d_consts = (D, D.count)
    p.emit("scalar", lambda e: e.activation(nega[:], nega[:], AF.Exp),
           waits=[d_consts], inc=S)
    p.emit("vector", lambda e: e.tensor_scalar(nega[:], nega[:], -1.0, None, ALU.mult),
           waits=[(S, S.count)], inc=V)

    rhsD = sb1("rhsD", [96, INNER], BF16)
    lhsTD = sb1("lhsTD", [96, 128], BF16)
    wstag = sb1("wstag", [128, 2 * 512], F32)
    qsf = sb1("qsf", [128, 512], F32)
    qsf8 = sb1("qsf8", [128, 2 * 512], FP8)
    rflat = qsf

    p.emit("gpsimd", lambda e: e.dma_start(out=utb[:], in_=cut_i[:]), inc=C)
    p.emit("gpsimd", lambda e: e.dma_start(out=lhsTD[:], in_=clhsd_i[:]), inc=C)
    p.emit("gpsimd", lambda e: e.dma_start(
        out=rhsD[:].rearrange("p (q c) -> p q c", q=8),
        in_=crhsd_i[:].rearrange("p (q c) -> p q c", q=8)), inc=C)
    p.emit("gpsimd", lambda e: e.dma_start(
        out=dbb[:].rearrange("p (q c) -> p q c", q=4),
        in_=db_i[:].rearrange("p (q c) -> p q c", q=4)), inc=C)
    casts_d = (C, C.count)

    wslot_g = {0: None, 1: None}

    def stage_piece(src, slot, cast_fn, queue):
        w = [wslot_g[slot]] if wslot_g[slot] else []
        ch = W if queue == "sync" else TP
        p.emit(queue, lambda e, s=src, sl=slot: e.dma_start(
            out=wstag[0:s.shape[0], sl * 512: sl * 512 + s.shape[1]], in_=s),
            waits=w, inc=ch)
        wv = (ch, ch.count)
        p.emit("vector", lambda e, sl=slot: cast_fn(e, sl),
               waits=[wv, d_consts], inc=V)
        wslot_g[slot] = (V, V.count)

    pi = 0
    for kc in range(8):
        for q in range(16):
            q0 = q * 274
            def cast_wi(e, sl, kc=kc, q0=q0):
                return e.tensor_scalar(
                    wiTb[:, kc * DP + q0: kc * DP + q0 + 274],
                    wstag[:, sl * 512: sl * 512 + 274],
                    normw[:, kc:kc + 1], None, ALU.mult)
            stage_piece(wiT[kc * 128:(kc + 1) * 128, q0:q0 + 274],
                        pi % 2, cast_wi, "sync" if pi % 4 < 2 else "scalar")
            pi += 1
    wiTb_ready = (V, V.count)
    p.emit("gpsimd", lambda e: e.dma_start(
        out=woTb[:].rearrange("p (k d) -> p k d", k=16),
        in_=woT[:].rearrange("(k p) d -> p k d", p=128)), inc=C)
    woTb_ready = (C, C.count)

    p.emit("vector", lambda e: e.memset(racc[:], 0.0), inc=V)
    quant_tasks = []
    q8_g = {0: None, 1: None}
    scol_ready = [None]

    def mk_p1task(src, slot, wi):
        def t():
            w = [wslot_g[slot]] if wslot_g[slot] else []
            p.emit("sync", lambda e, s=src, sl=slot: e.dma_start(
                out=wstag[:, sl * 512: sl * 512 + 512], in_=s),
                waits=w, inc=W)
            wv = (W, W.count)
            p.emit("vector", lambda e, sl=slot: e.tensor_reduce(
                rtmp[:], wstag[:, sl * 512: sl * 512 + 512], AX.X, ALU.add,
                apply_absolute_value=True), waits=[wv], inc=V, wb=True)
            p.emit("vector", lambda e, wi=wi: e.tensor_tensor(
                racc[:, wi:wi + 1], racc[:, wi:wi + 1], rtmp[:], ALU.add),
                waits=(), inc=V)
            wslot_g[slot] = (V, V.count)
        return t

    def t_scale():
        p.emit("gpsimd", lambda e: e.dma_start(
            out=rflat[0:1, :].rearrange("o (p c) -> o p c", p=128), in_=racc[:]),
            waits=[(V, V.count), (S, S.count), (D, D.count)], inc=D)
        dv = (D, D.count)
        for wi in range(3):
            p.emit("vector", lambda e, wi=wi: e.tensor_reduce(
                rflat[0:1, 508 + wi:508 + wi + 1],
                rflat[0:1, :].rearrange("o (p c) -> o c p", c=4)[:, wi, :],
                AX.X, ALU.add), waits=[dv], inc=V, wb=True)
        p.emit("vector", lambda e: e.tensor_scalar(
            rflat[0:1, 504:507], rflat[0:1, 508:511], 1.0 / (DM * INNER), 1e-5,
            ALU.mult, ALU.max), waits=(), inc=V)
        p.emit("vector", lambda e: e.reciprocal(rflat[0:1, 500:503],
                                                rflat[0:1, 504:507]),
               waits=(), inc=V, wb=True)
        vv = (V, V.count)
        p.emit("gpsimd", lambda e: e.dma_start(out=bscr[0:8],
                                               in_=rflat[0:1, 500:508]),
               waits=[vv], inc=D)
        dv = (D, D.count)
        p.emit("gpsimd", lambda e: e.dma_start(
            out=mabs[:], in_=bscr[4:8].unsqueeze(0).broadcast_to([128, 4])),
            waits=[dv], inc=D)
        p.emit("gpsimd", lambda e: e.dma_start(
            out=scol[:], in_=bscr[0:4].unsqueeze(0).broadcast_to([128, 4])),
            waits=(), inc=D)
        scol_ready[0] = (D, D.count)

    def mk_p2task(src, dst, slot, wi):
        def t():
            w = [wslot_g[slot]] if wslot_g[slot] else []
            p.emit("sync", lambda e, s=src, sl=slot: e.dma_start(
                out=wstag[:, sl * 512: sl * 512 + 512], in_=s),
                waits=w, inc=W)
            wv = (W, W.count)
            p.emit("vector", lambda e, sl=slot, wi=wi: e.tensor_scalar(
                qsf[:], wstag[:, sl * 512: sl * 512 + 512],
                scol[:, wi:wi + 1], MAGIC, ALU.mult, ALU.add),
                waits=[wv, scol_ready[0]], inc=V)
            p.emit("vector", lambda e: e.tensor_scalar(
                qsf[:], qsf[:], MAGIC, 1.0, ALU.subtract, ALU.min),
                waits=(), inc=V)
            wq8 = [q8_g[slot]] if q8_g[slot] else []
            p.emit("vector", lambda e, sl=slot: e.tensor_scalar(
                qsf8[:, sl * 512:(sl + 1) * 512], qsf[:], -1.0, None,
                ALU.max), waits=wq8, inc=V)
            wslot_g[slot] = (V, V.count)
            vv = (V, V.count)
            p.emit("gpsimd", lambda e, d=dst, sl=slot: e.dma_start(
                out=d, in_=qsf8[:, sl * 512:(sl + 1) * 512]),
                waits=[vv], inc=Q)
            q8_g[slot] = (Q, Q.count)
        return t

    qslot = [0]
    for wsrc, wi in [(wgT, 0), (wuT, 1), (wdT, 2)]:
        rows, cols = wsrc.shape
        for pr in range(rows // 128):
            for pc in range(cols // 512):
                quant_tasks.append(mk_p1task(
                    wsrc[pr * 128:(pr + 1) * 128, pc * 512:(pc + 1) * 512],
                    qslot[0] % 2, wi))
                qslot[0] += 1
    quant_tasks.append(t_scale)
    for wsrc, wdst, wi in [(wgT, wq_g, 0), (wuT, wq_u, 1), (wdT, wq_d, 2)]:
        rows, cols = wsrc.shape
        for pr in range(rows // 128):
            for pc in range(cols // 512):
                quant_tasks.append(mk_p2task(
                    wsrc[pr * 128:(pr + 1) * 128, pc * 512:(pc + 1) * 512],
                    wdst[pr * 128:(pr + 1) * 128, pc * 512:(pc + 1) * 512],
                    qslot[0] % 2, wi))
                qslot[0] += 1

    def run_tasks(n):
        for _ in range(min(n, len(quant_tasks))):
            quant_tasks.pop(0)()

    xring = sb1("xring", [128, 3 * DM], F32)
    t0ring = sb1("t0ring", [128, 2 * DM], BF16)
    xnscr = sb1("xnscr", [128, DM], BF16)
    xnT = sb1("xnT", [128, 8 * TC], BF16)
    szb = sb1("szb", [128, 16 * TC], BF16)
    CONVW_T = TC + 4
    xssmb = sb1("xssmb", [128, 16 * CONVW_T], BF16)
    Bb = sb1("Bb", [128, TC], BF16)
    Cb = sb1("Cb", [128, TC], BF16)
    dtraw = sb1("dtraw", [32, TC], F32)
    cacc = sb1("cacc", [128, 128], F32)
    gacc = sb1("gacc", [128, 128], F32)
    XCfm = sb1("XCfm", [128, 8 * 128], BF16)
    XC = sb1("XC", [128, DI], BF16)
    X2 = sb1("X2", [128, DI], BF16)
    fsave = sb1("fsave", [128, 32], BF16)
    Etile = sb1("Etile", [128, NH * 128], BF16)
    emf = sb1("emf", [128, 512], F32)
    smm = sb1("smm", [128, 128], BF16)
    btm2 = sb1("btm2", [128, 256], BF16)
    dts = [sb1(f"dts{i}", [32, 128], F32) for i in range(2)]
    dtv = sb1("dtv", [32, 128], F32)
    av_t = sb1("av", [32, 128], F32)
    ac_t = sb1("ac", [32, 128], F32)
    ea0 = sb1("ea0", [32, 128], F32)
    zer32 = sb1("zer32", [32, 128], BF16)
    hibt = sb1("hibt", [32, 128], BF16)
    lobt = sb1("lobt", [32, 128], BF16)
    dtT = sb1("dtT", [128, 32], F32)
    ea0t = sb1("ea0t", [128, 32], F32)
    datb = sb1("datb", [128, 32], F32)
    ytile = sb1("ytile", [128, DI], BF16)
    yT = sb1("yT", [128, DI], BF16)
    htile = sb1("htile", [128, DI], F32)
    hbt = sb1("hbt", [128, DI], BF16)
    smu = tiny("smu", sb1); svar = tiny("svar", sb1); srstd = tiny("srstd", sb1)
    srr = tiny("srr", sb1); ssum = tiny("ssum", sb1); ssq = tiny("ssq", sb1)
    amu = tiny("amu", sb1); avar = tiny("avar", sb1); arstd = tiny("arstd", sb1)
    asum = tiny("asum", sb1); asq = tiny("asq", sb1)
    tscr = tiny("tscr", sb1)

    p.emit("vector", lambda e: e.memset(zer32[:], 0.0), inc=V)
    p.emit("vector", lambda e: e.memset(htile[:], 0.0), inc=V)
    p.emit("vector", lambda e: e.memset(hbt[:], 0.0), inc=V)
    hbt_ready = (V, V.count)

    def p1_stats(g):
        slot = g % 3
        xs = xring[:, slot * DM:(slot + 1) * DM]
        p.emit("scalar", lambda e, xs=xs, g=g: e.activation(
            xnscr[:], xs, AF.Square, accum_out=ssq[:, g:g + 1]),
            waits=[(D, D.count), (T, T.count)], inc=S, wb=True)
        p.emit("vector", lambda e, xs=xs, g=g: e.tensor_reduce(
            ssum[:, g:g + 1], xs, AX.X, ALU.add), waits=[(D, D.count)], inc=V, wb=True)
        p.emit("vector", lambda e, g=g: e.tensor_scalar(
            smu[:, g:g + 1], ssum[:, g:g + 1], 1.0 / DM, None, ALU.mult),
            waits=(), inc=V)
        p.emit("vector", lambda e, g=g: e.tensor_tensor(
            tscr[:, g:g + 1], smu[:, g:g + 1], smu[:, g:g + 1], ALU.mult),
            waits=(), inc=V)
        p.emit("vector", lambda e, g=g: e.tensor_scalar(
            svar[:, g:g + 1], ssq[:, g:g + 1], 1.0 / DM, None, ALU.mult),
            waits=[(S, S.count)], inc=V)
        p.emit("vector", lambda e, g=g: e.tensor_tensor(
            svar[:, g:g + 1], svar[:, g:g + 1], tscr[:, g:g + 1], ALU.subtract),
            waits=(), inc=V)
        vv = (V, V.count)
        emit_rstd([vv], svar[:, g:g + 1], srstd[:, g:g + 1], EPS_LN)
        p.emit("vector", lambda e, g=g: e.tensor_tensor(
            tscr[:, g:g + 1], srstd[:, g:g + 1], srstd[:, g:g + 1], ALU.mult),
            waits=[(S, S.count)], inc=V)
        p.emit("vector", lambda e, g=g: e.tensor_tensor(
            tscr[:, g:g + 1], tscr[:, g:g + 1], svar[:, g:g + 1], ALU.mult),
            waits=(), inc=V)
        vv = (V, V.count)
        p.emit("vector", lambda e, g=g: e.tensor_scalar(
            tscr[:, g:g + 1], tscr[:, g:g + 1], EPS_RMS, None, ALU.add),
            waits=[vv], inc=V)
        vv2 = (V, V.count)
        p.emit("scalar", lambda e, g=g: e.activation(
            tscr[:, g:g + 1], tscr[:, g:g + 1], AF.Ln),
            waits=[vv2], inc=S)
        p.emit("scalar", lambda e, g=g: e.activation(
            tscr[:, g:g + 1], tscr[:, g:g + 1], AF.Exp, scale=-0.5),
            waits=(), inc=S)
        p.emit("vector", lambda e, g=g: e.tensor_tensor(
            srr[:, g:g + 1], srstd[:, g:g + 1], tscr[:, g:g + 1], ALU.mult),
            waits=[(S, S.count)], inc=V)

    for g in (0, 1):
        slot = g % 3
        p.emit("gpsimd", lambda e, g=g, slot=slot: e.dma_start(
            out=xring[:, slot * DM:(slot + 1) * DM],
            in_=x_in[g * 128:(g + 1) * 128, :]), waits=(), inc=D)
        p1_stats(g)

    bankg = {i: [] for i in range(8)}
    szb_free = None
    for c in range(n_p1_chunks):
        g0 = 2 * c
        subs = (g0, g0 + 1)

        for g in subs:
            slot = g % 3
            xs = xring[:, slot * DM:(slot + 1) * DM]
            t0 = t0ring[:, (g % 2) * DM:(g % 2 + 1) * DM]
            p.emit("vector", lambda e, xs=xs, t0=t0, g=g: e.tensor_scalar(
                t0, xs, smu[:, g:g + 1], srstd[:, g:g + 1],
                ALU.subtract, ALU.mult), waits=[(S, S.count)], inc=V)
            p.emit("vector", lambda e, xs=xs, g=g: e.tensor_scalar(
                xnscr[:], xs, smu[:, g:g + 1], srr[:, g:g + 1],
                ALU.subtract, ALU.mult), waits=[(T, T.count)], inc=V)
            vv = (V, V.count)
            sub = g - g0
            p.emit("scalar", lambda e, sub=sub: e.dma_start_transpose(
                xnT[:].rearrange("p (kc t) -> p kc t", kc=8)[:, :, sub * 128:(sub + 1) * 128],
                xnscr[:].rearrange("t (kc f) -> t kc f", kc=8)),
                waits=[vv], inc=T)

        if c == 0:
            p.emit("gpsimd", lambda e: e.memset(
                xssmb[:].rearrange("p (fc t) -> p fc t", fc=16)[:, :, 0:4], 0.0),
                waits=[(V, V.count)], inc=G)
        else:
            p.emit("gpsimd", lambda e: e.tensor_copy(
                xssmb[:].rearrange("p (fc t) -> p fc t", fc=16)[:, :, 1:4],
                xssmb[:].rearrange("p (fc t) -> p fc t", fc=16)[:, :, TC + 1:TC + 4]),
                waits=[(V, V.count)], inc=G)
        halo_g = (G, G.count)

        tw = (T, T.count)
        for j in range(N_CHUNK_W):
            jw = 128 if j < 34 else 32
            bank = P[j % 4]

            def mm_inproj(e, j=j, jw=jw, bank=bank):
                last = None
                for k in range(8):
                    last = e.matmul(
                        bank[0:jw, 0:TC],
                        wiTb[:, k * DP + j * 128: k * DP + j * 128 + jw],
                        xnT[:, k * TC:(k + 1) * TC],
                        start=(k == 0), stop=(k == 7))
                return last
            p.emit("tensor", mm_inproj,
                   waits=[tw] + bankg[j % 4]
                   + ([wiTb_ready] if c == 0 else []), inc=E)
            ev = (E, E.count)
            if j < 16:
                p.emit("scalar", lambda e, j=j, bank=bank: e.activation(
                    szb[:, j * TC:(j + 1) * TC], bank[:, 0:TC], AF.Silu),
                    waits=[ev] + ([szb_free] if szb_free else []), inc=S)
                bankg[j % 4] = [(S, S.count)]
            elif j < 32:
                fc = j - 16
                p.emit("scalar", lambda e, fc=fc, bank=bank: e.activation(
                    xssmb[:, fc * CONVW_T + 4: fc * CONVW_T + 4 + TC],
                    bank[:, 0:TC], AF.Copy),
                    waits=[ev, halo_g], inc=S)
                bankg[j % 4] = [(S, S.count)]
            elif j == 32:
                p.emit("vector", lambda e, bank=bank: e.tensor_copy(
                    Bb[:], bank[:, 0:TC]),
                    waits=[ev, (TB, TB.count)], inc=V)
                bankg[j % 4] = [(V, V.count)]
                bb_ready = (V, V.count)
            elif j == 33:
                p.emit("vector", lambda e, bank=bank: e.tensor_copy(
                    Cb[:], bank[:, 0:TC]), waits=[ev], inc=V)
                bankg[j % 4] = [(V, V.count)]
                cb_ready = (V, V.count)
            else:
                p.emit("vector", lambda e, bank=bank: e.tensor_scalar(
                    dtraw[:], bank[0:32, 0:TC], dtb[:], None, ALU.add),
                    waits=[ev], inc=V)
                bankg[j % 4] = [(V, V.count)]
        run_tasks(8)

        if c + 1 < NT1:
            g = g0 + 2
            slot = g % 3
            p.emit("gpsimd", lambda e, g=g, slot=slot: e.dma_start(
                out=xring[:, slot * DM:(slot + 1) * DM],
                in_=x_in[g * 128:(g + 1) * 128, :]),
                waits=[(V, V.count), (D, D.count)], inc=D)

        p.emit("scalar", lambda e: e.dma_start_transpose(
            btm2[:].rearrange("p (s t) -> p s t", s=2),
            Bb[:].rearrange("n (s t) -> n s t", s=2)),
            waits=[bb_ready] + (prev_state_e if c > 0 else []), inc=TB)
        btm_t = (TB, TB.count)

        for g in subs:
            sub = g - g0
            for hf in range(2):
                conv_eng = "vector"
                conv_ch = V
                acc = cacc if hf == 0 else gacc
                for fci in range(8):
                    fc = hf * 8 + fci
                    base = fc * CONVW_T + 4 + sub * 128

                    def conv_fc(e, fc=fc, base=base, acc=acc):
                        e.tensor_scalar(acc[:], xssmb[:, base - 3: base - 3 + 128],
                                        convw[:, fc * KC:fc * KC + 1],
                                        convb[:, fc:fc + 1], ALU.mult, ALU.add)
                        last = None
                        for k in range(1, KC):
                            last = e.scalar_tensor_tensor(
                                acc[:], xssmb[:, base - 3 + k: base - 3 + k + 128],
                                convw[:, fc * KC + k:fc * KC + k + 1], acc[:],
                                ALU.mult, ALU.add)
                        return last
                    p.emit(conv_eng, conv_fc, waits=[(S, S.count)], inc=conv_ch)
                    cw = (conv_ch, conv_ch.count)
                    p.emit("scalar", lambda e, fci=fci, acc=acc: e.activation(
                        XCfm[:, fci * 128:(fci + 1) * 128], acc[:], AF.Silu),
                        waits=[cw, (TX, TX.count)], inc=S)
                sv = (S, S.count)
                p.emit("scalar", lambda e, hf=hf: e.dma_start_transpose(
                    XC[:, hf * 1024:(hf + 1) * 1024].rearrange(
                        "p (fc t) -> p fc t", fc=8),
                    XCfm[:].rearrange("t (fc f) -> t fc f", fc=8)),
                    waits=[sv, (V, V.count), (E, E.count)], inc=TX)
            xc_t = (TX, TX.count)

            dsl = dtraw[:, sub * 128:(sub + 1) * 128]
            p.emit("scalar", lambda e, dsl=dsl: e.activation(dts[0][:], dsl, AF.Abs),
                   waits=[(V, V.count)], inc=S)
            p.emit("scalar", lambda e: e.activation(dts[1][:], dts[0][:], AF.Exp,
                                                    scale=-1.0), waits=(), inc=S)
            sv = (S, S.count)
            p.emit("vector", lambda e: e.tensor_scalar(dts[1][:], dts[1][:], 1.0,
                                                       None, ALU.add),
                   waits=[sv], inc=V)
            vv = (V, V.count)
            p.emit("scalar", lambda e: e.activation(dts[1][:], dts[1][:], AF.Ln),
                   waits=[vv], inc=S)
            p.emit("vector", lambda e, dsl=dsl: e.tensor_relu(dts[0][:], dsl),
                   waits=[(S, S.count)], inc=V)
            p.emit("vector", lambda e: e.tensor_tensor(dtv[:], dts[0][:], dts[1][:],
                                                       ALU.add), waits=(), inc=V)
            p.emit("vector", lambda e: e.tensor_scalar(av_t[:], dtv[:], nega[:],
                                                       None, ALU.mult),
                   waits=(), inc=V)
            p.emit("vector", lambda e: e.tensor_tensor_scan(
                ac_t[:], av_t[:], zer32[:], 0.0, ALU.add, ALU.bypass),
                waits=(), inc=V, wb=True)
            vv = (V, V.count)
            p.emit("scalar", lambda e: e.activation(ea0[:], ac_t[:], AF.Exp),
                   waits=[vv, (D, D.count)], inc=S)
            p.emit("vector", lambda e: e.tensor_copy(hibt[:], ac_t[:]),
                   waits=[(TR, TR.count)], inc=V)
            p.emit("vector", lambda e: e.tensor_copy(dts[0][:], hibt[:]),
                   waits=(), inc=V)
            p.emit("vector", lambda e: e.tensor_tensor(lobt[:], ac_t[:], dts[0][:],
                                                       ALU.subtract), waits=(), inc=V)
            p.emit("vector", lambda e: e.tensor_scalar(
                lhsTD[0:32, :], hibt[:], -1.0, None, ALU.mult),
                waits=[(E, E.count), casts_d], inc=V)
            p.emit("vector", lambda e: e.tensor_scalar(
                lhsTD[32:64, :], lobt[:], -1.0, None, ALU.mult), waits=(), inc=V)
            lhsTD_ready = (V, V.count)
            vv = (V, V.count)
            p.emit("scalar", lambda e: e.dma_start(out=rhsD[64:65, :], in_=hibt[:]),
                   waits=[vv, (E, E.count), casts_d], inc=TR)
            p.emit("scalar", lambda e: e.dma_start(out=rhsD[65:66, :], in_=lobt[:]),
                   waits=(), inc=TR)
            flat_t = (TR, TR.count)

            vv = (V, V.count)
            p.emit("tensor", lambda e: e.transpose(P[4][0:128, 0:32], dtv[:],
                                                   idf[0:32, 0:32]),
                   waits=[vv, (S, S.count)] + bankg[4], inc=E)
            sv = (S, S.count)
            p.emit("tensor", lambda e: e.transpose(P[4][0:128, 32:64], ea0[:],
                                                   idf[0:32, 0:32]),
                   waits=[sv], inc=E)
            p.emit("tensor", lambda e, sub=sub: e.matmul(
                P[4][0:128, 192:320], Bb[:, sub * 128:(sub + 1) * 128],
                Cb[:, sub * 128:(sub + 1) * 128], start=True, stop=True),
                waits=[cb_ready], inc=E)
            ev = (E, E.count)
            p.emit("vector", lambda e: e.tensor_copy(dtT[:], P[4][0:128, 0:32]),
                   waits=[ev, (D, D.count)], inc=V)
            p.emit("vector", lambda e: e.tensor_copy(ea0t[:], P[4][0:128, 32:64]),
                   waits=(), inc=V)

            p.emit("vector", lambda e: e.tensor_tensor(smm[:], P[4][0:128, 192:320],
                                                       utb[:], ALU.mult),
                   waits=[casts_d], inc=V)
            bankg[4] = [(V, V.count)]
            sv2 = (S, S.count)
            p.emit("gpsimd", lambda e: e.dma_start(out=dscr[0:32],
                                                   in_=ea0[:, 127:128]),
                   waits=[sv2, (V, V.count), (D, D.count)], inc=D)
            p.emit("gpsimd", lambda e: e.dma_start(
                out=datb[:], in_=dscr[0:32].unsqueeze(0).broadcast_to([128, 32])),
                waits=[(D, D.count)], inc=D)
            datb_d = (D, D.count)

            for eo in range(8):
                bank = P[5 + eo % 2]
                p.emit("tensor", lambda e, eo=eo, bank=bank: e.matmul(
                    bank[:, 0:512], lhsTD[:, :], rhsD[:, eo * 512:(eo + 1) * 512],
                    start=True, stop=True),
                    waits=([flat_t, lhsTD_ready, casts_d] if eo == 0 else [])
                    + bankg[5 + eo % 2], inc=E)
                ev = (E, E.count)
                half = emf[:, 0:512]
                p.emit("vector", lambda e, half=half, bank=bank: e.tensor_scalar_min(
                    half, bank[:, 0:512], 0.0), waits=[ev, (S, S.count)], inc=V)
                bankg[5 + eo % 2] = [(V, V.count)]
                vv = (V, V.count)
                p.emit("scalar", lambda e, eo=eo, half=half: e.activation(
                    Etile[:, eo * 512:(eo + 1) * 512], half, AF.Exp),
                    waits=[vv, (E, E.count)], inc=S)
            sv_e = (S, S.count)

            dtT_b = dtT[:].unsqueeze(2).broadcast_to([128, 32, 64])
            p.emit("vector", lambda e, b=dtT_b: e.tensor_tensor(
                X2[:].rearrange("p (h d) -> p h d", h=32), XC[:].rearrange(
                    "p (h d) -> p h d", h=32), b, ALU.mult),
                waits=[xc_t, (E, E.count)], inc=V)
            fcol = Etile[:].rearrange("p (h t) -> p h t", h=32)[:, :, 127]
            p.emit("vector", lambda e, f=fcol: e.tensor_copy(fsave[:], f),
                   waits=[sv_e], inc=V)
            smm_b = smm[:].unsqueeze(1).broadcast_to([128, 32, 128])
            p.emit("vector", lambda e, b=smm_b: e.tensor_tensor(
                Etile[:].rearrange("p (h t) -> p h t", h=32),
                Etile[:].rearrange("p (h t) -> p h t", h=32), b, ALU.mult),
                waits=(), inc=V)
            p.emit("vector", lambda e: e.tensor_tensor(XC[:], XC[:], dbb[:],
                                                       ALU.mult),
                waits=[casts_d], inc=V)
            ms_v = (V, V.count)

            for grp in range(4):
                bank = P[7 if grp % 2 == 0 else 4]

                def mm_y2(e, grp=grp, bank=bank, sub=sub):
                    last = None
                    for hh in range(8):
                        h = grp * 8 + hh
                        last = e.matmul(
                            bank[:, hh * 64:(hh + 1) * 64],
                            Cb[:, sub * 128:(sub + 1) * 128],
                            hbt[:, h * 64:(h + 1) * 64], start=True, stop=True)
                    return last
                p.emit("tensor", mm_y2,
                       waits=[cb_ready, hbt_ready] + bankg[7 if grp % 2 == 0 else 4],
                       inc=E)
                ev = (E, E.count)
                ea_b = ea0t[:, grp * 8:(grp + 1) * 8].unsqueeze(2).broadcast_to(
                    [128, 8, 64])
                p.emit("vector", lambda e, grp=grp, bank=bank, b=ea_b: e.tensor_tensor(
                    ytile[:, grp * 512:(grp + 1) * 512].rearrange(
                        "p (h d) -> p h d", h=8),
                    bank[:, 0:512].rearrange("p (h d) -> p h d", h=8), b, ALU.mult),
                    waits=[ev, (TY, TY.count)], inc=V)
                bankg[7 if grp % 2 == 0 else 4] = [(V, V.count)]
            for grp in range(4):
                bank = P[5 if grp % 2 == 0 else 6]

                def mm_y1(e, grp=grp, bank=bank):
                    last = None
                    for hh in range(8):
                        h = grp * 8 + hh
                        last = e.matmul(
                            bank[:, hh * 64:(hh + 1) * 64],
                            Etile[:, h * 128:(h + 1) * 128],
                            X2[:, h * 64:(h + 1) * 64], start=True, stop=True)
                    return last
                p.emit("tensor", mm_y1,
                       waits=[ms_v] + bankg[5 if grp % 2 == 0 else 6], inc=E)
                ev = (E, E.count)
                p.emit("vector", lambda e, grp=grp, bank=bank: e.scalar_tensor_tensor(
                    ytile[:, grp * 512:(grp + 1) * 512],
                    ytile[:, grp * 512:(grp + 1) * 512], 1.0, bank[:, 0:512],
                    ALU.mult, ALU.add), waits=[ev], inc=V)
                bankg[5 if grp % 2 == 0 else 6] = [(V, V.count)]
                p.emit("vector", lambda e, grp=grp: e.tensor_tensor(
                    ytile[:, grp * 512:(grp + 1) * 512],
                    ytile[:, grp * 512:(grp + 1) * 512],
                    XC[:, grp * 512:(grp + 1) * 512], ALU.add), waits=(), inc=V)
            vv = (V, V.count)
            p.emit("scalar", lambda e: e.dma_start_transpose(
                yT[:].rearrange("p (fc t) -> p fc t", fc=16),
                ytile[:].rearrange("t (fc f) -> t fc f", fc=16)),
                waits=[vv, (E, E.count)], inc=TY)
            yt_t = (TY, TY.count)
            sz_ap = szb[:].rearrange("p (fc t) -> p fc t", fc=16)[
                :, :, sub * 128:(sub + 1) * 128]
            p.emit("vector", lambda e, a=sz_ap: e.tensor_tensor(
                yT[:].rearrange("p (fc t) -> p fc t", fc=16),
                yT[:].rearrange("p (fc t) -> p fc t", fc=16), a, ALU.mult),
                waits=[yt_t], inc=V)
            szb_free = (V, V.count)
            vv = (V, V.count)

            def mm_oproj(e):
                last = None
                for k in range(16):
                    e.matmul(P[0][:, 0:512], yT[:, k * 128:(k + 1) * 128],
                             woTb[:, k * DM: k * DM + 512],
                             start=(k == 0), stop=(k == 15))
                    last = e.matmul(P[1][:, 0:512], yT[:, k * 128:(k + 1) * 128],
                                    woTb[:, k * DM + 512: (k + 1) * DM],
                                    start=(k == 0), stop=(k == 15))
                return last
            p.emit("tensor", mm_oproj,
                   waits=[vv] + bankg[0] + bankg[1]
                   + ([woTb_ready] if c == 0 and sub == 0 else []), inc=E)
            ev = (E, E.count)
            slot = g % 3
            t0 = t0ring[:, (g % 2) * DM:(g % 2 + 1) * DM]
            p.emit("vector", lambda e, t0=t0: e.tensor_tensor(
                P[0][:, 0:512], P[0][:, 0:512], t0[:, 0:512], ALU.add),
                waits=[ev, (S, S.count)], inc=V)
            p.emit("vector", lambda e, t0=t0: e.tensor_tensor(
                P[1][:, 0:512], P[1][:, 0:512], t0[:, 512:1024], ALU.add),
                waits=(), inc=V)
            vv = (V, V.count)
            p.emit("scalar", lambda e, g=g: e.activation(
                xnscr[:, 0:512], P[0][:, 0:512], AF.Square,
                accum_out=asq[:, g:g + 1]), waits=[vv], inc=S, wb=True)
            p.emit("scalar", lambda e, g=g: e.activation(
                xnscr[:, 512:1024], P[1][:, 0:512], AF.Square,
                accum_out=tscr[:, g:g + 1]), waits=(), inc=S, wb=True)
            p.emit("vector", lambda e, g=g: e.tensor_reduce(
                asum[:, g:g + 1], P[0][:, 0:512], AX.X, ALU.add),
                waits=[(S, S.count)], inc=V, wb=True)
            p.emit("vector", lambda e, g=g: e.tensor_reduce(
                avar[:, g:g + 1], P[1][:, 0:512], AX.X, ALU.add), waits=(), inc=V, wb=True)
            p.emit("vector", lambda e, g=g: e.tensor_tensor(
                asum[:, g:g + 1], asum[:, g:g + 1], avar[:, g:g + 1], ALU.add),
                waits=(), inc=V)
            sv = (S, S.count)
            p.emit("vector", lambda e, g=g: e.tensor_tensor(
                asq[:, g:g + 1], asq[:, g:g + 1], tscr[:, g:g + 1], ALU.add),
                waits=[sv], inc=V)
            p.emit("vector", lambda e, g=g: e.tensor_scalar(
                amu[:, g:g + 1], asum[:, g:g + 1], 1.0 / DM, None, ALU.mult),
                waits=(), inc=V)
            p.emit("vector", lambda e, g=g: e.tensor_tensor(
                tscr[:, g:g + 1], amu[:, g:g + 1], amu[:, g:g + 1], ALU.mult),
                waits=(), inc=V)
            p.emit("vector", lambda e, g=g: e.tensor_scalar(
                avar[:, g:g + 1], asq[:, g:g + 1], 1.0 / DM, None, ALU.mult),
                waits=(), inc=V)
            p.emit("vector", lambda e, g=g: e.tensor_tensor(
                avar[:, g:g + 1], avar[:, g:g + 1], tscr[:, g:g + 1], ALU.subtract),
                waits=(), inc=V)
            vv = (V, V.count)
            emit_rstd([vv], avar[:, g:g + 1], arstd[:, g:g + 1], EPS_LN)
            sv = (S, S.count)
            xs = xring[:, slot * DM:(slot + 1) * DM]
            p.emit("vector", lambda e, g=g: e.tensor_scalar(
                P[0][:, 0:512], P[0][:, 0:512], amu[:, g:g + 1], arstd[:, g:g + 1],
                ALU.subtract, ALU.mult), waits=[sv], inc=V)
            p.emit("vector", lambda e, g=g: e.tensor_scalar(
                P[1][:, 0:512], P[1][:, 0:512], amu[:, g:g + 1], arstd[:, g:g + 1],
                ALU.subtract, ALU.mult), waits=(), inc=V)
            p.emit("vector", lambda e, xs=xs: e.tensor_tensor(
                xs[:, 0:512], xs[:, 0:512], P[0][:, 0:512], ALU.add),
                waits=(), inc=V)
            p.emit("vector", lambda e, xs=xs: e.tensor_tensor(
                xs[:, 512:1024], xs[:, 512:1024], P[1][:, 0:512], ALU.add),
                waits=(), inc=V)
            bankg[0] = [(V, V.count)]
            bankg[1] = [(V, V.count)]
            vv = (V, V.count)
            p.emit("gpsimd", lambda e, g=g, xs=xs: e.dma_start(
                out=x1s[g * 128:(g + 1) * 128, :], in_=xs), waits=[vv], inc=D)
            if sub == 0 and c + 1 < NT1:
                gn = g0 + 3
                nslot = gn % 3
                p.emit("gpsimd", lambda e, gn=gn, nslot=nslot: e.dma_start(
                    out=xring[:, nslot * DM:(nslot + 1) * DM],
                    in_=x_in[gn * 128:(gn + 1) * 128, :]),
                    waits=[(D, D.count)], inc=D)

            fb = fsave[:].unsqueeze(2).broadcast_to([128, 32, 64])
            p.emit("vector", lambda e, fb=fb: e.tensor_tensor(
                X2[:].rearrange("p (h d) -> p h d", h=32),
                X2[:].rearrange("p (h d) -> p h d", h=32), fb, ALU.mult),
                waits=[(E, E.count)], inc=V)
            for grp in range(4):
                bank = P[5 if grp % 2 == 0 else 6]

                def mm_state(e, grp=grp, bank=bank, sub=sub):
                    last = None
                    for hh in range(8):
                        h = grp * 8 + hh
                        last = e.matmul(
                            bank[:, hh * 64:(hh + 1) * 64],
                            btm2[:, sub * 128:(sub + 1) * 128],
                            X2[:, h * 64:(h + 1) * 64], start=True, stop=True)
                    return last
                p.emit("tensor", mm_state,
                       waits=[(V, V.count), btm_t]
                       + bankg[5 if grp % 2 == 0 else 6], inc=E)
                ev = (E, E.count)
                da_b = datb[:, grp * 8:(grp + 1) * 8].unsqueeze(2).broadcast_to(
                    [128, 8, 64])
                hsl = htile[:, grp * 512:(grp + 1) * 512]
                p.emit("vector", lambda e, hsl=hsl, b=da_b: e.tensor_tensor(
                    hsl.rearrange("p (h d) -> p h d", h=8),
                    hsl.rearrange("p (h d) -> p h d", h=8), b, ALU.mult),
                    waits=[datb_d, ev], inc=V)
                p.emit("vector", lambda e, hsl=hsl, bank=bank: e.tensor_tensor(
                    hsl, hsl, bank[:, 0:512], ALU.add), waits=(), inc=V)
                bankg[5 if grp % 2 == 0 else 6] = [(V, V.count)]
                vv = (V, V.count)
                p.emit("scalar", lambda e, grp=grp, hsl=hsl: e.activation(
                    hbt[:, grp * 512:(grp + 1) * 512], hsl, AF.Copy),
                    waits=[vv], inc=S)
            hbt_ready = (S, S.count)
            prev_state_e = [(E, E.count)]
            run_tasks(8)

        if c + 1 < n_p1_chunks:
            for g in (g0 + 2, g0 + 3):
                p1_stats(g)
    run_tasks(len(quant_tasks))

    if not do_p2:
        p1stack.close()
        _finalize(p, nc, stack, dram, dumps, locals())
        return nc

    p1stack.close()
    p.fence(ALL)
    wgq = sb("wgq", [128, 8 * INNER], FP8)
    wuq = sb("wuq", [128, 8 * INNER], FP8)
    wdq = sb("wdq", [128, 32 * DM], FP8)
    x1ring = sb("x1ring", [128, 4 * DM], F32)
    xn1 = sb("xn1", [128, DM], F32)
    xqf = sb("xqf", [128, DM], F32)
    xqb2 = sb("xqb2", [128, 2 * DM], BF16)
    xqT2 = sb("xqT2", [128, 2 * DM], BF16)
    ga2 = sb("ga2", [128, 2 * INNER], BF16)
    ua2 = sb("ua2", [128, 2 * INNER], BF16)
    p2dum = sb("p2dum", [128, DM], F32)
    tqf = sb("tqf", [128, INNER], F32)
    tqb = sb("tqb", [128, INNER], BF16)
    tqT = sb("tqT", [128, INNER], BF16)
    dfull = sb("dfull", [128, 2 * DM], F32)
    mu2 = tiny("mu2"); var2 = tiny("var2"); rstd2c = tiny("rstd2c")
    sum2 = tiny("sum2"); sq2 = tiny("sq2")
    amax1 = tiny("amax1"); sx1 = tiny("sx1"); vgc = tiny("vgc"); vuc = tiny("vuc")
    amax2 = tiny("amax2"); sx2 = tiny("sx2"); vdc = tiny("vdc")
    mud = tiny("mud"); vard = tiny("vard"); rstdd = tiny("rstdd")
    sumd = tiny("sumd"); sqd = tiny("sqd")
    t2scr = tiny("t2scr")

    barrier = p.barrier_waits(ALL)
    p.emit("sync", lambda e: e.dma_start(
        out=wgq[:].rearrange("p (k c) -> p k c", k=8),
        in_=wq_g[:].rearrange("(k p) c -> p k c", p=128)), waits=barrier, inc=W)
    p.emit("scalar", lambda e: e.dma_start(
        out=wuq[:].rearrange("p (k c) -> p k c", k=8),
        in_=wq_u[:].rearrange("(k p) c -> p k c", p=128)), waits=barrier, inc=U2)
    p.emit("gpsimd", lambda e: e.dma_start(
        out=wdq[:].rearrange("p (k c) -> p k c", k=32),
        in_=wq_d[:].rearrange("(k p) c -> p k c", p=128)), waits=barrier, inc=U3)
    wq_w = [(W, W.count), (U2, U2.count)]
    wd_w = [(U3, U3.count)]

    def p2_stats(cc):
        slot = cc % 4
        xs = x1ring[:, slot * DM:(slot + 1) * DM]
        p.emit("scalar", lambda e, xs=xs, cc=cc: e.activation(
            p2dum[:], xs, AF.Square, accum_out=sq2[:, cc % 16:cc % 16 + 1]),
            waits=[(D, D.count)], inc=S, wb=True)
        p.emit("vector", lambda e, xs=xs, cc=cc: e.tensor_reduce(
            sum2[:, cc % 16:cc % 16 + 1], xs, AX.X, ALU.add),
            waits=[(D, D.count)], inc=V, wb=True)
        i = cc % 16
        p.emit("vector", lambda e, i=i: e.tensor_scalar(
            mu2[:, i:i + 1], sum2[:, i:i + 1], 1.0 / DM, None, ALU.mult),
            waits=(), inc=V)
        p.emit("vector", lambda e, i=i: e.tensor_tensor(
            t2scr[:, i:i + 1], mu2[:, i:i + 1], mu2[:, i:i + 1], ALU.mult),
            waits=(), inc=V)
        sv = (S, S.count)
        p.emit("vector", lambda e, i=i: e.tensor_scalar(
            var2[:, i:i + 1], sq2[:, i:i + 1], 1.0 / DM, None, ALU.mult),
            waits=[sv], inc=V)
        p.emit("vector", lambda e, i=i: e.tensor_tensor(
            var2[:, i:i + 1], var2[:, i:i + 1], t2scr[:, i:i + 1], ALU.subtract),
            waits=(), inc=V)
        vv = (V, V.count)
        emit_rstd([vv], var2[:, i:i + 1], rstd2c[:, i:i + 1], EPS_LN)

    p.emit("gpsimd", lambda e: e.dma_start(
        out=x1ring[:, 0:DM], in_=x1s[0:128, :]), waits=barrier, inc=D)
    p2_stats(0)

    def p2_finalize(cc):
        i = cc % 16
        dsl = dfull[:, (cc % 2) * DM:(cc % 2 + 1) * DM]
        slot = cc % 4
        xs = x1ring[:, slot * DM:(slot + 1) * DM]
        sv = (S, S.count)
        p.emit("vector", lambda e, dsl=dsl, i=i: e.tensor_scalar(
            xn1[:], dsl, mud[:, i:i + 1], rstdd[:, i:i + 1],
            ALU.subtract, ALU.mult), waits=[sv, (D, D.count)], inc=V)
        p.emit("vector", lambda e, xs=xs: e.tensor_tensor(
            xn1[:], xn1[:], xs, ALU.add), waits=(), inc=V)
        vv = (V, V.count)
        p.emit("gpsimd", lambda e, cc=cc: e.dma_start(
            out=out[cc * 128:(cc + 1) * 128, :], in_=xn1[:]), waits=[vv], inc=D)

    dn_bankg = []
    prod_guard = {0: [], 1: []}
    gu_done = {0: [], 1: []}
    dn_done = []
    tq_ts = {}
    tq2_ts = {}

    def p2_front(cc):
        nonlocal dn_done
        i = cc % 16
        pc = cc % 2
        slot = cc % 4
        xs = x1ring[:, slot * DM:(slot + 1) * DM]
        xqb = xqb2[:, pc * DM:(pc + 1) * DM]
        xqT = xqT2[:, pc * DM:(pc + 1) * DM]
        ga = ga2[:, pc * INNER:(pc + 1) * INNER]
        ua = ua2[:, pc * INNER:(pc + 1) * INNER]
        if cc + 1 < NT2:
            nslot = (cc + 1) % 4
            p.emit("gpsimd", lambda e, cc=cc, nslot=nslot: e.dma_start(
                out=x1ring[:, nslot * DM:(nslot + 1) * DM],
                in_=x1s[(cc + 1) * 128:(cc + 2) * 128, :]),
                waits=[(V, V.count)], inc=D)
            p2_stats(cc + 1)
        sv = (S, S.count)
        p.emit("vector", lambda e, xs=xs, i=i: e.tensor_scalar(
            xn1[:], xs, mu2[:, i:i + 1], rstd2c[:, i:i + 1],
            ALU.subtract, ALU.mult), waits=[sv, (D, D.count)], inc=V)
        p.emit("vector", lambda e, i=i: e.tensor_reduce(
            amax1[:, i:i + 1], xn1[:], AX.X, ALU.max, apply_absolute_value=True),
            waits=(), inc=V, wb=True)
        p.emit("vector", lambda e, i=i: e.tensor_scalar(
            amax1[:, i:i + 1], amax1[:, i:i + 1], 1e-5, None, ALU.max),
            waits=(), inc=V, wb=True)
        p.emit("vector", lambda e, i=i: e.reciprocal(
            sx1[:, i:i + 1], amax1[:, i:i + 1]), waits=(), inc=V, wb=True)
        p.emit("vector", lambda e, i=i: e.tensor_scalar(
            sx1[:, i:i + 1], sx1[:, i:i + 1], 127.0, None, ALU.mult),
            waits=(), inc=V)
        p.emit("vector", lambda e, i=i: e.tensor_scalar(
            t2scr[:, i:i + 1], amax1[:, i:i + 1], 1.0 / 127.0, None, ALU.mult),
            waits=(), inc=V, wb=True)
        p.emit("vector", lambda e, i=i: e.tensor_tensor(
            vgc[:, i:i + 1], t2scr[:, i:i + 1], mabs[:, 0:1], ALU.mult),
            waits=(), inc=V)
        p.emit("vector", lambda e, i=i: e.tensor_tensor(
            vuc[:, i:i + 1], t2scr[:, i:i + 1], mabs[:, 1:2], ALU.mult),
            waits=(), inc=V)
        p.emit("vector", lambda e, i=i: e.tensor_scalar(
            xqf[:], xn1[:], sx1[:, i:i + 1], MAGIC, ALU.mult, ALU.add),
            waits=(), inc=V)
        p.emit("vector", lambda e, xqb=xqb: e.tensor_scalar(
            xqb[:], xqf[:], MAGIC, None, ALU.subtract),
               waits=[(S, S.count), (TX, TX.count)], inc=V)
        vv = (V, V.count)
        p.emit("scalar", lambda e, xqT=xqT, xqb=xqb: e.dma_start_transpose(
            xqT[:].rearrange("p (k t) -> p k t", k=8),
            xqb[:].rearrange("t (k f) -> t k f", k=8)),
            waits=[vv] + gu_done[pc], inc=TX)
        tq_ts[cc] = (TX, TX.count)
        vsc_ready = (V, V.count)

        for q in range(8):
            gb = P[q % 3]
            ub = P[3 + q % 3]
            col = q * 512

            def mm_g(e, col=col, gb=gb, xqT=xqT):
                last = None
                for k in range(8):
                    last = e.matmul(gb[:, 0:512], xqT[:, k * 128:(k + 1) * 128],
                                    wgq[:, k * INNER + col: k * INNER + col + 512],
                                    start=(k == 0), stop=(k == 7))
                return last

            def mm_u(e, col=col, ub=ub, xqT=xqT):
                last = None
                for k in range(8):
                    last = e.matmul(ub[:, 0:512], xqT[:, k * 128:(k + 1) * 128],
                                    wuq[:, k * INNER + col: k * INNER + col + 512],
                                    start=(k == 0), stop=(k == 7))
                return last
            p.emit("tensor", mm_g,
                   waits=([tq_ts[cc]] + wq_w if q == 0 else [])
                   + bankg[q % 3] + [(S, S.count)], inc=E)
            ev = (E, E.count)
            p.emit("scalar", lambda e, col=col, gb=gb, i=i, ga=ga: e.activation(
                ga[:, col:col + 512], gb[:, 0:512], AF.Silu,
                scale=vgc[:, i:i + 1]),
                waits=[ev, vsc_ready] + prod_guard[pc], inc=S)
            bankg[q % 3] = [(S, S.count)]
            p.emit("tensor", mm_u,
                   waits=bankg[3 + q % 3] + [(S, S.count)], inc=E)
            ev = (E, E.count)
            p.emit("scalar", lambda e, col=col, ub=ub, i=i, ua=ua: e.activation(
                ua[:, col:col + 512], ub[:, 0:512], AF.Copy,
                scale=vuc[:, i:i + 1]), waits=[ev], inc=S)
            bankg[3 + q % 3] = [(S, S.count)]
        gu_done[pc] = [(E, E.count)]

    def p2_back(cc):
        nonlocal dn_done, dn_bankg
        i = cc % 16
        pc = cc % 2
        ga = ga2[:, pc * INNER:(pc + 1) * INNER]
        ua = ua2[:, pc * INNER:(pc + 1) * INNER]
        sv = (S, S.count)
        p.emit("vector", lambda e, ga=ga, ua=ua: e.tensor_tensor(
            tqf[:], ga[:], ua[:], ALU.mult), waits=[sv], inc=V)
        prod_guard[pc] = [(V, V.count)]
        p.emit("vector", lambda e, i=i: e.tensor_reduce(
            amax2[:, i:i + 1], tqf[:], AX.X, ALU.max, apply_absolute_value=True),
            waits=(), inc=V, wb=True)
        p.emit("vector", lambda e, i=i: e.tensor_scalar(
            amax2[:, i:i + 1], amax2[:, i:i + 1], 1e-5, None, ALU.max),
            waits=(), inc=V, wb=True)
        p.emit("vector", lambda e, i=i: e.reciprocal(
            sx2[:, i:i + 1], amax2[:, i:i + 1]), waits=(), inc=V, wb=True)
        p.emit("vector", lambda e, i=i: e.tensor_scalar(
            sx2[:, i:i + 1], sx2[:, i:i + 1], 127.0, None, ALU.mult),
            waits=(), inc=V)
        p.emit("vector", lambda e, i=i: e.tensor_scalar(
            t2scr[:, i:i + 1], amax2[:, i:i + 1], 1.0 / 127.0, None, ALU.mult),
            waits=(), inc=V, wb=True)
        p.emit("vector", lambda e, i=i: e.tensor_tensor(
            vdc[:, i:i + 1], t2scr[:, i:i + 1], mabs[:, 2:3], ALU.mult),
            waits=(), inc=V)
        p.emit("vector", lambda e, i=i: e.tensor_scalar(
            tqf[:], tqf[:], sx2[:, i:i + 1], MAGIC, ALU.mult, ALU.add),
            waits=(), inc=V)
        p.emit("vector", lambda e: e.tensor_scalar(
            tqb[:], tqf[:], MAGIC, None, ALU.subtract),
            waits=[(TY, TY.count)], inc=V)
        vv = (V, V.count)
        p.emit("scalar", lambda e: e.dma_start_transpose(
            tqT[:].rearrange("p (k t) -> p k t", k=32),
            tqb[:].rearrange("t (k f) -> t k f", k=32)),
            waits=[vv] + dn_done, inc=TY)
        tq2_ts[cc] = (TY, TY.count)

        def mm_down(e):
            last = None
            for k in range(32):
                lhs = tqT[:, k * 128:(k + 1) * 128]
                e.matmul(P[6][:, 0:512], lhs, wdq[:, k * DM: k * DM + 512],
                         start=(k == 0), stop=(k == 31))
                last = e.matmul(P[7][:, 0:512], lhs,
                                wdq[:, k * DM + 512:(k + 1) * DM],
                                start=(k == 0), stop=(k == 31))
            return last
        p.emit("tensor", mm_down,
               waits=[tq2_ts[cc], (S, S.count)] + wd_w + dn_bankg, inc=E)
        dn_done = [(E, E.count)]
        ev = (E, E.count)
        dsl = dfull[:, (cc % 2) * DM:(cc % 2 + 1) * DM]
        p.emit("vector", lambda e, dsl=dsl, i=i: e.tensor_scalar(
            dsl[:, 0:512], P[6][:, 0:512], vdc[:, i:i + 1], None, ALU.mult),
            waits=[ev], inc=V)
        p.emit("vector", lambda e, dsl=dsl, i=i: e.tensor_scalar(
            dsl[:, 512:1024], P[7][:, 0:512], vdc[:, i:i + 1], None, ALU.mult),
            waits=(), inc=V)
        dn_bankg = [(V, V.count)]
        vv = (V, V.count)
        p.emit("scalar", lambda e, dsl=dsl, i=i: e.activation(
            p2dum[:], dsl, AF.Square, accum_out=sqd[:, i:i + 1]),
            waits=[vv], inc=S, wb=True)
        p.emit("vector", lambda e, dsl=dsl, i=i: e.tensor_reduce(
            sumd[:, i:i + 1], dsl, AX.X, ALU.add), waits=(), inc=V, wb=True)
        if cc > 0:
            ip = (cc - 1) % 16
            sv = (S, S.count)
            p.emit("vector", lambda e, ip=ip: e.tensor_scalar(
                mud[:, ip:ip + 1], sumd[:, ip:ip + 1], 1.0 / DM, None, ALU.mult),
                waits=[sv], inc=V)
            p.emit("vector", lambda e, ip=ip: e.tensor_tensor(
                t2scr[:, ip:ip + 1], mud[:, ip:ip + 1], mud[:, ip:ip + 1],
                ALU.mult), waits=(), inc=V)
            p.emit("vector", lambda e, ip=ip: e.tensor_scalar(
                vard[:, ip:ip + 1], sqd[:, ip:ip + 1], 1.0 / DM, None, ALU.mult),
                waits=(), inc=V)
            p.emit("vector", lambda e, ip=ip: e.tensor_tensor(
                vard[:, ip:ip + 1], vard[:, ip:ip + 1], t2scr[:, ip:ip + 1],
                ALU.subtract), waits=(), inc=V)
            vv = (V, V.count)
            emit_rstd([vv], vard[:, ip:ip + 1], rstdd[:, ip:ip + 1], EPS_LN)
            p2_finalize(cc - 1)

    P2_PIPE = True
    if P2_PIPE:
        for cc in range(NT2 + 1):
            if cc < NT2:
                p2_front(cc)
            if cc >= 1:
                p2_back(cc - 1)
    else:
        for cc in range(NT2):
            p2_front(cc)
            p2_back(cc)

    ip = (NT2 - 1) % 16
    sv = (S, S.count)
    p.emit("vector", lambda e, ip=ip: e.tensor_scalar(
        mud[:, ip:ip + 1], sumd[:, ip:ip + 1], 1.0 / DM, None, ALU.mult),
        waits=[sv], inc=V)
    p.emit("vector", lambda e, ip=ip: e.tensor_tensor(
        t2scr[:, ip:ip + 1], mud[:, ip:ip + 1], mud[:, ip:ip + 1], ALU.mult),
        waits=(), inc=V)
    p.emit("vector", lambda e, ip=ip: e.tensor_scalar(
        vard[:, ip:ip + 1], sqd[:, ip:ip + 1], 1.0 / DM, None, ALU.mult),
        waits=(), inc=V)
    p.emit("vector", lambda e, ip=ip: e.tensor_tensor(
        vard[:, ip:ip + 1], vard[:, ip:ip + 1], t2scr[:, ip:ip + 1], ALU.subtract),
        waits=(), inc=V)
    vv = (V, V.count)
    emit_rstd([vv], vard[:, ip:ip + 1], rstdd[:, ip:ip + 1], EPS_LN)
    p2_finalize(NT2 - 1)

    _finalize(p, nc, stack, dram, dumps, locals())
    return nc


def _finalize(p, nc, stack, dram, dumps, env):
    Dch = env["D"]
    allch = env["ALL"]
    for name in dumps:
        tile = env[name]
        shape = list(tile[:].shape)
        dd = nc.dram_tensor("dbg_" + name, shape, tile[:].dtype,
                            kind="ExternalOutput").ap()
        p.emit("gpsimd", lambda e, t=tile, d=dd: e.dma_start(out=d, in_=t[:]),
               waits=p.barrier_waits(allch), inc=Dch)
    p.emit("gpsimd", lambda e: e.wait_ge(Dch.sem, Dch.count), waits=(), inc=None)
    with stack:
        block = stack.enter_context(nc.Block())
        p.replay(block, None)


_CACHE = {}


def _host_consts():
    hm = np.zeros((96, INNER), np.float32)
    for h in range(32):
        hm[h, h * 128:(h + 1) * 128] = 1.0
        hm[32 + h, h * 128:(h + 1) * 128] = 1.0
    lhsd = np.zeros((96, 128), np.float32)
    lhsd[64, :] = 1.0
    lhsd[65, :] = 1.0
    cid = np.eye(32, dtype=np.float32)
    ut = np.triu(np.ones((128, 128), np.float32))
    return hm, lhsd, cid, ut


def kernel(L0_attn_x, L0_attn_norm_w, L0_attn_in_proj_w, L0_attn_conv_w,
           L0_attn_conv_b, L0_attn_A_log, L0_attn_D, L0_attn_dt_bias,
           L0_attn_out_proj_w, L0_ffn_w_gate, L0_ffn_w_up, L0_ffn_w_down):
    x = np.ascontiguousarray(np.asarray(L0_attn_x, dtype=np.float32))
    wiT = np.ascontiguousarray(np.asarray(L0_attn_in_proj_w, np.float32).T)
    woT = np.ascontiguousarray(np.asarray(L0_attn_out_proj_w, np.float32).T)
    wgT = np.ascontiguousarray(np.asarray(L0_ffn_w_gate, np.float32).T)
    wuT = np.ascontiguousarray(np.asarray(L0_ffn_w_up, np.float32).T)
    wdT = np.ascontiguousarray(np.asarray(L0_ffn_w_down, np.float32).T)
    normw = np.ascontiguousarray(
        np.asarray(L0_attn_norm_w, np.float32).reshape(8, 128).T)
    convw = np.ascontiguousarray(np.asarray(L0_attn_conv_w, np.float32)
                                 .reshape(16, 128, KC).transpose(1, 0, 2)
                                 .reshape(128, 16 * KC))
    convb = np.ascontiguousarray(
        np.asarray(L0_attn_conv_b, np.float32).reshape(16, 128).T)
    alog = np.ascontiguousarray(np.asarray(L0_attn_A_log, np.float32)
                                .reshape(32, 1))
    dtb = np.ascontiguousarray(np.asarray(L0_attn_dt_bias, np.float32)
                               .reshape(32, 1))
    db = np.ascontiguousarray(np.broadcast_to(
        np.repeat(np.asarray(L0_attn_D, np.float32), HD)[None, :], (128, DI)))
    hm, lhsd, cid, ut = _host_consts()

    if "nc" not in _CACHE:
        _CACHE["nc"] = build_program()
    nc = _CACHE["nc"]

    base = dict(wiT=wiT, woT=woT, wgT=wgT, wuT=wuT, wdT=wdT, normw=normw,
                convw=convw, convb=convb, alog=alog, dtb=dtb, db=db,
                c_id=cid, c_ut=ut, c_rhsd=hm, c_lhsd=lhsd)
    in_maps = []
    for c in range(8):
        m = dict(base)
        m["x_in"] = np.ascontiguousarray(x[c])
        in_maps.append(m)
    res = run_bass_kernel_spmd(nc, in_maps, core_ids=list(range(8)))
    return np.stack([np.asarray(r["out"]) for r in res.results], axis=0)


# version 09 (final)
# speedup vs baseline: 1.1744x; 1.1744x over previous
import numpy as np
from contextlib import ExitStack

import concourse.bass as bass
import concourse.mybir as mybir
from concourse.bass_utils import run_bass_kernel_spmd
from concourse.bass_isa import ReduceOp

F32 = mybir.dt.float32
BF16 = mybir.dt.bfloat16
I16 = mybir.dt.int16
I8 = mybir.dt.int8
FP8 = mybir.dt.float8e4
AF = mybir.ActivationFunctionType
ALU = mybir.AluOpType
AX = mybir.AxisListType

L = 2048
DM = 1024
DI = 2048
DS = 128
NH = 32
HD = 64
DP = 4384
KC = 4
TC = 256
NT1 = L // TC
SUB = 128
INNER = 4096
NT2 = L // SUB
EPS_LN = 1e-5
EPS_RMS = 1e-6
N_CHUNK_W = 35
MAGIC = 12582912.0


class Chan:
    def __init__(self, stack, nc, name, dma=False, owner=None):
        self.sem = stack.enter_context(nc.semaphore(name))
        self.count = 0
        self.step = 16 if dma else 1
        self.name = name
        self.owner = owner


EMIT_BUDGET = None
DROP_SELF_WAITS = False
AUTO_SELF_WAIT = True
SERIAL_FROM = None


class Plan:
    ENGINES = ("sync", "scalar", "vector", "tensor", "gpsimd")

    def __init__(self):
        self.ops = {e: [] for e in self.ENGINES}
        self.total = 0
        self.all_chans = None
        self.eng_chan = None
        self._waited = {e: {} for e in self.ENGINES}
        self._fence = {e: [] for e in self.ENGINES}

    def fence(self, chans):
        for e in self.ENGINES:
            self._fence[e] = [(c, c.count) for c in chans]

    def emit(self, eng, fn, waits=(), inc=None):
        if EMIT_BUDGET is not None and self.total >= EMIT_BUDGET:
            return
        if SERIAL_FROM is not None and self.total >= SERIAL_FROM \
                and self.all_chans is not None:
            waits = list(waits) + [(c, c.count) for c in self.all_chans]
        if AUTO_SELF_WAIT and self.eng_chan is not None:
            ch = self.eng_chan.get(eng)
            if ch is not None:
                waits = list(waits) + [(ch, ch.count)]
        self.total += 1
        wl = []
        pend = self._fence[eng]
        if pend:
            waits = list(pend) + list(waits)
            self._fence[eng] = []
        for w in waits:
            if isinstance(w, tuple):
                ch, val = w
            else:
                ch, val = w, w.count
            if val <= 0:
                continue
            if DROP_SELF_WAITS and ch.owner == eng:
                continue
            prev = self._waited[eng].get(ch.name, 0)
            if val > prev:
                wl.append((ch, val))
                self._waited[eng][ch.name] = val
        if inc is not None:
            inc.count += inc.step
        self.ops[eng].append((fn, wl, inc))

    def barrier_waits(self, chans):
        return [(c, c.count) for c in chans]

    def replay(self, block, engines):
        for name in self.ENGINES:
            ops = self.ops[name]

            def body(e, ops=ops):
                for fn, wl, inc in ops:
                    for ch, val in wl:
                        e.wait_ge(ch.sem, val)
                    instr = fn(e)
                    if inc is not None:
                        if instr is None:
                            e.sem_inc(inc.sem, inc.step)
                        else:
                            instr.then_inc(inc.sem, inc.step)
            getattr(block, name)(body)


def build_program(n_p1_chunks=NT1, do_p2=True, dumps=(), x1s_out=False,
                  detect_races=False):
    nc = bass.Bass("TRN2", target_bir_lowering=False,
                   detect_race_conditions=detect_races)

    dram = {}

    def din(name, shape, dt=F32):
        dram[name] = nc.dram_tensor(name, shape, dt, kind="ExternalInput").ap()
        return dram[name]

    def dout(name, shape, dt=F32):
        dram[name] = nc.dram_tensor(name, shape, dt, kind="ExternalOutput").ap()
        return dram[name]

    def dint(name, shape, dt=F32):
        dram[name] = nc.dram_tensor(name, shape, dt, kind="Internal").ap()
        return dram[name]

    x_in = din("x_in", [L, DM])
    wiT = din("wiT", [DM, DP])
    woT = din("woT", [DI, DM])
    wgT = din("wgT", [DM, INNER])
    wuT = din("wuT", [DM, INNER])
    wdT = din("wdT", [INNER, DM])
    normw_i = din("normw", [128, 8])
    convw_i = din("convw", [128, 16 * KC])
    convb_i = din("convb", [128, 16])
    alog_i = din("alog", [32, 1])
    dtb_i = din("dtb", [32, 1])
    db_i = din("db", [128, DI])
    cid_i = din("c_id", [32, 32])
    cut_i = din("c_ut", [128, 128])
    crhsd_i = din("c_rhsd", [96, INNER])
    clhsd_i = din("c_lhsd", [96, 128])

    out = dout("out", [L, DM])
    x1s = (dout if x1s_out else dint)("x1s", [L, DM])
    wq_g = dint("wq_g", [DM, INNER], FP8)
    wq_u = dint("wq_u", [DM, INNER], FP8)
    wq_d = dint("wq_d", [INNER, DM], FP8)
    bscr = dint("bscr", [64])
    dscr = dint("dscr", [32])

    stack = ExitStack()
    sb = lambda name, shape, dt: stack.enter_context(nc.sbuf_tensor(name, shape, dt))
    ps = lambda name: stack.enter_context(nc.psum_tensor(name, [128, 512], F32))

    W = Chan(stack, nc, "chW", dma=True)
    T = Chan(stack, nc, "chT", dma=True)
    D = Chan(stack, nc, "chD", dma=True)
    C = Chan(stack, nc, "chC", dma=True)
    Q = Chan(stack, nc, "chQ", dma=True)
    U2 = Chan(stack, nc, "chU2", dma=True)
    U3 = Chan(stack, nc, "chU3", dma=True)
    TP = Chan(stack, nc, "chTP", dma=True)
    TX = Chan(stack, nc, "chTX", dma=True)
    TY = Chan(stack, nc, "chTY", dma=True)
    TB = Chan(stack, nc, "chTB", dma=True)
    TR = Chan(stack, nc, "chTR", dma=True)
    S = Chan(stack, nc, "chS", owner="scalar")
    V = Chan(stack, nc, "chV", owner="vector")
    E = Chan(stack, nc, "chE", owner="tensor")
    G = Chan(stack, nc, "chG", owner="gpsimd")
    ALL = (W, T, D, C, Q, U2, U3, TP, TX, TY, TB, TR, S, V, E, G)

    p = Plan()
    p.all_chans = ALL
    p.eng_chan = {"vector": V, "scalar": S, "gpsimd": G}

    mabs = sb("mabs", [128, 4], F32)
    scol = sb("scol", [128, 4], F32)
    racc = sb("racc", [128, 4], F32)
    rtmp = sb("rtmp", [128, 1], F32)
    p1stack = ExitStack()
    sb1 = lambda name, shape, dt: p1stack.enter_context(
        nc.sbuf_tensor(name, shape, dt))
    idf = sb1("idf", [32, 32], F32)
    utb = sb1("utb", [128, 128], BF16)
    normw = sb1("normw_t", [128, 8], F32)
    convw = sb1("convw_t", [128, 16 * KC], F32)
    convb = sb1("convb_t", [128, 16], F32)
    nega = sb1("nega", [32, 1], F32)
    dtb = sb1("dtb_t", [32, 1], F32)
    dbb = sb1("dbb", [128, DI], BF16)

    P = [ps(f"P{i}") for i in range(8)]

    def tiny(name, fn=None):
        return (fn or sb)(name, [128, 16], F32)

    def emit_rstd(win_eng_waits, var_col, out_col, eps):
        p.emit("vector", lambda e, a=var_col, b=out_col, ep=eps: e.tensor_scalar(
            b, a, ep, None, ALU.add), waits=win_eng_waits, inc=V)
        vv = (V, V.count)
        p.emit("scalar", lambda e, b=out_col: e.activation(
            b, b, AF.Ln), waits=[vv], inc=S)
        p.emit("scalar", lambda e, b=out_col: e.activation(
            b, b, AF.Exp, scale=-0.5), waits=(), inc=S)

    wiTb = sb1("wiTb", [128, 8 * DP], BF16)
    woTb = sb1("woTb", [128, 16 * DM], BF16)

    def dma(eng_ch, eng, dst, src):
        p.emit(eng, lambda e, d=dst, s=src: e.dma_start(out=d, in_=s),
               waits=(), inc=eng_ch)

    cidf_w = D
    p.emit("gpsimd", lambda e: e.dma_start(out=idf[:], in_=cid_i[:]), inc=D)
    p.emit("gpsimd", lambda e: e.dma_start(out=normw[:], in_=normw_i[:]), inc=D)
    p.emit("gpsimd", lambda e: e.dma_start(out=convw[:], in_=convw_i[:]), inc=D)
    p.emit("gpsimd", lambda e: e.dma_start(out=convb[:], in_=convb_i[:]), inc=D)
    p.emit("gpsimd", lambda e: e.dma_start(out=nega[:], in_=alog_i[:]), inc=D)
    p.emit("gpsimd", lambda e: e.dma_start(out=dtb[:], in_=dtb_i[:]), inc=D)
    d_consts = (D, D.count)
    p.emit("scalar", lambda e: e.activation(nega[:], nega[:], AF.Exp),
           waits=[d_consts], inc=S)
    p.emit("vector", lambda e: e.tensor_scalar(nega[:], nega[:], -1.0, None, ALU.mult),
           waits=[(S, S.count)], inc=V)

    rhsD = sb1("rhsD", [96, INNER], BF16)
    lhsTD = sb1("lhsTD", [96, 128], BF16)
    wstag = sb1("wstag", [128, 2 * 512], F32)
    qsf = sb1("qsf", [128, 512], F32)
    qsf8 = sb1("qsf8", [128, 2 * 512], FP8)
    rflat = qsf

    p.emit("gpsimd", lambda e: e.dma_start(out=utb[:], in_=cut_i[:]), inc=C)
    p.emit("gpsimd", lambda e: e.dma_start(out=lhsTD[:], in_=clhsd_i[:]), inc=C)
    p.emit("gpsimd", lambda e: e.dma_start(
        out=rhsD[:].rearrange("p (q c) -> p q c", q=8),
        in_=crhsd_i[:].rearrange("p (q c) -> p q c", q=8)), inc=C)
    p.emit("gpsimd", lambda e: e.dma_start(
        out=dbb[:].rearrange("p (q c) -> p q c", q=4),
        in_=db_i[:].rearrange("p (q c) -> p q c", q=4)), inc=C)
    casts_d = (C, C.count)

    wslot_g = {0: None, 1: None}

    def stage_piece(src, slot, cast_fn, queue):
        w = [wslot_g[slot]] if wslot_g[slot] else []
        ch = W if queue == "sync" else TP
        p.emit(queue, lambda e, s=src, sl=slot: e.dma_start(
            out=wstag[0:s.shape[0], sl * 512: sl * 512 + s.shape[1]], in_=s),
            waits=w, inc=ch)
        wv = (ch, ch.count)
        p.emit("vector", lambda e, sl=slot: cast_fn(e, sl),
               waits=[wv, d_consts], inc=V)
        wslot_g[slot] = (V, V.count)

    pi = 0
    for kc in range(8):
        for q in range(16):
            q0 = q * 274
            def cast_wi(e, sl, kc=kc, q0=q0):
                return e.tensor_scalar(
                    wiTb[:, kc * DP + q0: kc * DP + q0 + 274],
                    wstag[:, sl * 512: sl * 512 + 274],
                    normw[:, kc:kc + 1], None, ALU.mult)
            stage_piece(wiT[kc * 128:(kc + 1) * 128, q0:q0 + 274],
                        pi % 2, cast_wi, "sync" if pi % 4 < 2 else "scalar")
            pi += 1
    wiTb_ready = (V, V.count)
    p.emit("gpsimd", lambda e: e.dma_start(
        out=woTb[:].rearrange("p (k d) -> p k d", k=16),
        in_=woT[:].rearrange("(k p) d -> p k d", p=128)), inc=C)
    woTb_ready = (C, C.count)

    p.emit("vector", lambda e: e.memset(racc[:], 0.0), inc=V)
    quant_tasks = []
    q8_g = {0: None, 1: None}
    scol_ready = [None]

    def mk_p1task(src, slot, wi):
        def t():
            w = [wslot_g[slot]] if wslot_g[slot] else []
            p.emit("sync", lambda e, s=src, sl=slot: e.dma_start(
                out=wstag[:, sl * 512: sl * 512 + 512], in_=s),
                waits=w, inc=W)
            wv = (W, W.count)
            p.emit("vector", lambda e, sl=slot: e.tensor_reduce(
                rtmp[:], wstag[:, sl * 512: sl * 512 + 512], AX.X, ALU.add,
                apply_absolute_value=True), waits=[wv], inc=V)
            p.emit("vector", lambda e, wi=wi: e.tensor_tensor(
                racc[:, wi:wi + 1], racc[:, wi:wi + 1], rtmp[:], ALU.add),
                waits=(), inc=V)
            wslot_g[slot] = (V, V.count)
        return t

    def t_scale():
        p.emit("gpsimd", lambda e: e.dma_start(
            out=rflat[0:1, :].rearrange("o (p c) -> o p c", p=128), in_=racc[:]),
            waits=[(V, V.count), (D, D.count)], inc=D)
        dv = (D, D.count)
        for wi in range(3):
            p.emit("vector", lambda e, wi=wi: e.tensor_reduce(
                rflat[0:1, 508 + wi:508 + wi + 1],
                rflat[0:1, :].rearrange("o (p c) -> o c p", c=4)[:, wi, :],
                AX.X, ALU.add), waits=[dv], inc=V)
        p.emit("vector", lambda e: e.tensor_scalar(
            rflat[0:1, 504:507], rflat[0:1, 508:511], 1.0 / (DM * INNER), 1e-5,
            ALU.mult, ALU.max), waits=(), inc=V)
        p.emit("vector", lambda e: e.reciprocal(rflat[0:1, 500:503],
                                                rflat[0:1, 504:507]),
               waits=(), inc=V)
        vv = (V, V.count)
        p.emit("gpsimd", lambda e: e.dma_start(out=bscr[0:8],
                                               in_=rflat[0:1, 500:508]),
               waits=[vv], inc=D)
        dv = (D, D.count)
        p.emit("gpsimd", lambda e: e.dma_start(
            out=mabs[:], in_=bscr[4:8].unsqueeze(0).broadcast_to([128, 4])),
            waits=[dv], inc=D)
        p.emit("gpsimd", lambda e: e.dma_start(
            out=scol[:], in_=bscr[0:4].unsqueeze(0).broadcast_to([128, 4])),
            waits=(), inc=D)
        scol_ready[0] = (D, D.count)

    def mk_p2task(src, dst, slot, wi):
        def t():
            w = [wslot_g[slot]] if wslot_g[slot] else []
            p.emit("sync", lambda e, s=src, sl=slot: e.dma_start(
                out=wstag[:, sl * 512: sl * 512 + 512], in_=s),
                waits=w, inc=W)
            wv = (W, W.count)
            p.emit("vector", lambda e, sl=slot, wi=wi: e.tensor_scalar(
                qsf[:], wstag[:, sl * 512: sl * 512 + 512],
                scol[:, wi:wi + 1], MAGIC, ALU.mult, ALU.add),
                waits=[wv, scol_ready[0]], inc=V)
            p.emit("vector", lambda e: e.tensor_scalar(
                qsf[:], qsf[:], MAGIC, 1.0, ALU.subtract, ALU.min),
                waits=(), inc=V)
            wq8 = [q8_g[slot]] if q8_g[slot] else []
            p.emit("vector", lambda e, sl=slot: e.tensor_scalar(
                qsf8[:, sl * 512:(sl + 1) * 512], qsf[:], -1.0, None,
                ALU.max), waits=wq8, inc=V)
            wslot_g[slot] = (V, V.count)
            vv = (V, V.count)
            p.emit("gpsimd", lambda e, d=dst, sl=slot: e.dma_start(
                out=d, in_=qsf8[:, sl * 512:(sl + 1) * 512]),
                waits=[vv], inc=Q)
            q8_g[slot] = (Q, Q.count)
        return t

    qslot = [0]
    for wsrc, wi in [(wgT, 0), (wuT, 1), (wdT, 2)]:
        rows, cols = wsrc.shape
        for pr in range(rows // 128):
            for pc in range(cols // 512):
                quant_tasks.append(mk_p1task(
                    wsrc[pr * 128:(pr + 1) * 128, pc * 512:(pc + 1) * 512],
                    qslot[0] % 2, wi))
                qslot[0] += 1
    quant_tasks.append(t_scale)
    for wsrc, wdst, wi in [(wgT, wq_g, 0), (wuT, wq_u, 1), (wdT, wq_d, 2)]:
        rows, cols = wsrc.shape
        for pr in range(rows // 128):
            for pc in range(cols // 512):
                quant_tasks.append(mk_p2task(
                    wsrc[pr * 128:(pr + 1) * 128, pc * 512:(pc + 1) * 512],
                    wdst[pr * 128:(pr + 1) * 128, pc * 512:(pc + 1) * 512],
                    qslot[0] % 2, wi))
                qslot[0] += 1

    def run_tasks(n):
        for _ in range(min(n, len(quant_tasks))):
            quant_tasks.pop(0)()

    xring = sb1("xring", [128, 3 * DM], F32)
    t0ring = sb1("t0ring", [128, 2 * DM], BF16)
    xnscr = sb1("xnscr", [128, DM], BF16)
    xnT = sb1("xnT", [128, 8 * TC], BF16)
    szb = sb1("szb", [128, 16 * TC], BF16)
    CONVW_T = TC + 4
    xssmb = sb1("xssmb", [128, 16 * CONVW_T], BF16)
    Bb = sb1("Bb", [128, TC], BF16)
    Cb = sb1("Cb", [128, TC], BF16)
    dtraw = sb1("dtraw", [32, TC], F32)
    cacc = sb1("cacc", [128, 128], F32)
    gacc = sb1("gacc", [128, 128], F32)
    XCfm = sb1("XCfm", [128, 8 * 128], BF16)
    XC = sb1("XC", [128, DI], BF16)
    X2 = sb1("X2", [128, DI], BF16)
    fsave = sb1("fsave", [128, 32], BF16)
    Etile = sb1("Etile", [128, NH * 128], BF16)
    emf = sb1("emf", [128, 512], F32)
    smm = sb1("smm", [128, 128], BF16)
    btm2 = sb1("btm2", [128, 256], BF16)
    dts = [sb1(f"dts{i}", [32, 128], F32) for i in range(2)]
    dtv = sb1("dtv", [32, 128], F32)
    av_t = sb1("av", [32, 128], F32)
    ac_t = sb1("ac", [32, 128], F32)
    ea0 = sb1("ea0", [32, 128], F32)
    zer32 = sb1("zer32", [32, 128], BF16)
    hibt = sb1("hibt", [32, 128], BF16)
    lobt = sb1("lobt", [32, 128], BF16)
    dtT = sb1("dtT", [128, 32], F32)
    ea0t = sb1("ea0t", [128, 32], F32)
    datb = sb1("datb", [128, 32], F32)
    ytile = sb1("ytile", [128, DI], BF16)
    yT = sb1("yT", [128, DI], BF16)
    htile = sb1("htile", [128, DI], F32)
    hbt = sb1("hbt", [128, DI], BF16)
    smu = tiny("smu", sb1); svar = tiny("svar", sb1); srstd = tiny("srstd", sb1)
    srr = tiny("srr", sb1); ssum = tiny("ssum", sb1); ssq = tiny("ssq", sb1)
    amu = tiny("amu", sb1); avar = tiny("avar", sb1); arstd = tiny("arstd", sb1)
    asum = tiny("asum", sb1); asq = tiny("asq", sb1)
    tscr = tiny("tscr", sb1)

    p.emit("vector", lambda e: e.memset(zer32[:], 0.0), inc=V)
    p.emit("vector", lambda e: e.memset(htile[:], 0.0), inc=V)
    p.emit("vector", lambda e: e.memset(hbt[:], 0.0), inc=V)
    hbt_ready = (V, V.count)

    def p1_stats(g):
        slot = g % 3
        xs = xring[:, slot * DM:(slot + 1) * DM]
        p.emit("scalar", lambda e, xs=xs, g=g: e.activation(
            xnscr[:], xs, AF.Square, accum_out=ssq[:, g:g + 1]),
            waits=[(D, D.count), (T, T.count)], inc=S)
        p.emit("vector", lambda e, xs=xs, g=g: e.tensor_reduce(
            ssum[:, g:g + 1], xs, AX.X, ALU.add), waits=[(D, D.count)], inc=V)
        p.emit("vector", lambda e, g=g: e.tensor_scalar(
            smu[:, g:g + 1], ssum[:, g:g + 1], 1.0 / DM, None, ALU.mult),
            waits=(), inc=V)
        p.emit("vector", lambda e, g=g: e.tensor_tensor(
            tscr[:, g:g + 1], smu[:, g:g + 1], smu[:, g:g + 1], ALU.mult),
            waits=(), inc=V)
        p.emit("vector", lambda e, g=g: e.tensor_scalar(
            svar[:, g:g + 1], ssq[:, g:g + 1], 1.0 / DM, None, ALU.mult),
            waits=[(S, S.count)], inc=V)
        p.emit("vector", lambda e, g=g: e.tensor_tensor(
            svar[:, g:g + 1], svar[:, g:g + 1], tscr[:, g:g + 1], ALU.subtract),
            waits=(), inc=V)
        vv = (V, V.count)
        emit_rstd([vv], svar[:, g:g + 1], srstd[:, g:g + 1], EPS_LN)
        p.emit("vector", lambda e, g=g: e.tensor_tensor(
            tscr[:, g:g + 1], srstd[:, g:g + 1], srstd[:, g:g + 1], ALU.mult),
            waits=[(S, S.count)], inc=V)
        p.emit("vector", lambda e, g=g: e.tensor_tensor(
            tscr[:, g:g + 1], tscr[:, g:g + 1], svar[:, g:g + 1], ALU.mult),
            waits=(), inc=V)
        vv = (V, V.count)
        p.emit("vector", lambda e, g=g: e.tensor_scalar(
            tscr[:, g:g + 1], tscr[:, g:g + 1], EPS_RMS, None, ALU.add),
            waits=[vv], inc=V)
        vv2 = (V, V.count)
        p.emit("scalar", lambda e, g=g: e.activation(
            tscr[:, g:g + 1], tscr[:, g:g + 1], AF.Ln),
            waits=[vv2], inc=S)
        p.emit("scalar", lambda e, g=g: e.activation(
            tscr[:, g:g + 1], tscr[:, g:g + 1], AF.Exp, scale=-0.5),
            waits=(), inc=S)
        p.emit("vector", lambda e, g=g: e.tensor_tensor(
            srr[:, g:g + 1], srstd[:, g:g + 1], tscr[:, g:g + 1], ALU.mult),
            waits=[(S, S.count)], inc=V)

    for g in (0, 1):
        slot = g % 3
        p.emit("gpsimd", lambda e, g=g, slot=slot: e.dma_start(
            out=xring[:, slot * DM:(slot + 1) * DM],
            in_=x_in[g * 128:(g + 1) * 128, :]), waits=(), inc=D)
        p1_stats(g)

    bankg = {i: [] for i in range(8)}
    szb_free = None
    for c in range(n_p1_chunks):
        g0 = 2 * c
        subs = (g0, g0 + 1)

        for g in subs:
            slot = g % 3
            xs = xring[:, slot * DM:(slot + 1) * DM]
            t0 = t0ring[:, (g % 2) * DM:(g % 2 + 1) * DM]
            p.emit("vector", lambda e, xs=xs, t0=t0, g=g: e.tensor_scalar(
                t0, xs, smu[:, g:g + 1], srstd[:, g:g + 1],
                ALU.subtract, ALU.mult), waits=[(S, S.count)], inc=V)
            p.emit("vector", lambda e, xs=xs, g=g: e.tensor_scalar(
                xnscr[:], xs, smu[:, g:g + 1], srr[:, g:g + 1],
                ALU.subtract, ALU.mult), waits=[(T, T.count)], inc=V)
            vv = (V, V.count)
            sub = g - g0
            p.emit("scalar", lambda e, sub=sub: e.dma_start_transpose(
                xnT[:].rearrange("p (kc t) -> p kc t", kc=8)[:, :, sub * 128:(sub + 1) * 128],
                xnscr[:].rearrange("t (kc f) -> t kc f", kc=8)),
                waits=[vv], inc=T)

        if c == 0:
            p.emit("gpsimd", lambda e: e.memset(
                xssmb[:].rearrange("p (fc t) -> p fc t", fc=16)[:, :, 0:4], 0.0),
                waits=[(V, V.count)], inc=G)
        else:
            p.emit("gpsimd", lambda e: e.tensor_copy(
                xssmb[:].rearrange("p (fc t) -> p fc t", fc=16)[:, :, 1:4],
                xssmb[:].rearrange("p (fc t) -> p fc t", fc=16)[:, :, TC + 1:TC + 4]),
                waits=[(V, V.count)], inc=G)
        halo_g = (G, G.count)

        tw = (T, T.count)
        for j in range(N_CHUNK_W):
            jw = 128 if j < 34 else 32
            bank = P[j % 4]

            def mm_inproj(e, j=j, jw=jw, bank=bank):
                last = None
                for k in range(8):
                    last = e.matmul(
                        bank[0:jw, 0:TC],
                        wiTb[:, k * DP + j * 128: k * DP + j * 128 + jw],
                        xnT[:, k * TC:(k + 1) * TC],
                        start=(k == 0), stop=(k == 7))
                return last
            p.emit("tensor", mm_inproj,
                   waits=[tw] + bankg[j % 4]
                   + ([wiTb_ready] if c == 0 else []), inc=E)
            ev = (E, E.count)
            if j < 16:
                p.emit("scalar", lambda e, j=j, bank=bank: e.activation(
                    szb[:, j * TC:(j + 1) * TC], bank[:, 0:TC], AF.Silu),
                    waits=[ev] + ([szb_free] if szb_free else []), inc=S)
                bankg[j % 4] = [(S, S.count)]
            elif j < 32:
                fc = j - 16
                p.emit("scalar", lambda e, fc=fc, bank=bank: e.activation(
                    xssmb[:, fc * CONVW_T + 4: fc * CONVW_T + 4 + TC],
                    bank[:, 0:TC], AF.Copy),
                    waits=[ev, halo_g], inc=S)
                bankg[j % 4] = [(S, S.count)]
            elif j == 32:
                p.emit("vector", lambda e, bank=bank: e.tensor_copy(
                    Bb[:], bank[:, 0:TC]),
                    waits=[ev, (TB, TB.count)], inc=V)
                bankg[j % 4] = [(V, V.count)]
                bb_ready = (V, V.count)
            elif j == 33:
                p.emit("vector", lambda e, bank=bank: e.tensor_copy(
                    Cb[:], bank[:, 0:TC]), waits=[ev], inc=V)
                bankg[j % 4] = [(V, V.count)]
                cb_ready = (V, V.count)
            else:
                p.emit("vector", lambda e, bank=bank: e.tensor_scalar(
                    dtraw[:], bank[0:32, 0:TC], dtb[:], None, ALU.add),
                    waits=[ev], inc=V)
                bankg[j % 4] = [(V, V.count)]
        run_tasks(8)

        if c + 1 < NT1:
            g = g0 + 2
            slot = g % 3
            p.emit("gpsimd", lambda e, g=g, slot=slot: e.dma_start(
                out=xring[:, slot * DM:(slot + 1) * DM],
                in_=x_in[g * 128:(g + 1) * 128, :]),
                waits=[(V, V.count), (D, D.count)], inc=D)

        p.emit("scalar", lambda e: e.dma_start_transpose(
            btm2[:].rearrange("p (s t) -> p s t", s=2),
            Bb[:].rearrange("n (s t) -> n s t", s=2)),
            waits=[bb_ready] + (prev_state_e if c > 0 else []), inc=TB)
        btm_t = (TB, TB.count)

        for g in subs:
            sub = g - g0
            for hf in range(2):
                conv_eng = "vector"
                conv_ch = V
                acc = cacc if hf == 0 else gacc
                for fci in range(8):
                    fc = hf * 8 + fci
                    base = fc * CONVW_T + 4 + sub * 128

                    def conv_fc(e, fc=fc, base=base, acc=acc):
                        e.tensor_scalar(acc[:], xssmb[:, base - 3: base - 3 + 128],
                                        convw[:, fc * KC:fc * KC + 1],
                                        convb[:, fc:fc + 1], ALU.mult, ALU.add)
                        last = None
                        for k in range(1, KC):
                            last = e.scalar_tensor_tensor(
                                acc[:], xssmb[:, base - 3 + k: base - 3 + k + 128],
                                convw[:, fc * KC + k:fc * KC + k + 1], acc[:],
                                ALU.mult, ALU.add)
                        return last
                    p.emit(conv_eng, conv_fc, waits=[(S, S.count)], inc=conv_ch)
                    cw = (conv_ch, conv_ch.count)
                    p.emit("scalar", lambda e, fci=fci, acc=acc: e.activation(
                        XCfm[:, fci * 128:(fci + 1) * 128], acc[:], AF.Silu),
                        waits=[cw, (TX, TX.count)], inc=S)
                sv = (S, S.count)
                p.emit("scalar", lambda e, hf=hf: e.dma_start_transpose(
                    XC[:, hf * 1024:(hf + 1) * 1024].rearrange(
                        "p (fc t) -> p fc t", fc=8),
                    XCfm[:].rearrange("t (fc f) -> t fc f", fc=8)),
                    waits=[sv, (V, V.count), (E, E.count)], inc=TX)
            xc_t = (TX, TX.count)

            dsl = dtraw[:, sub * 128:(sub + 1) * 128]
            p.emit("scalar", lambda e, dsl=dsl: e.activation(dts[0][:], dsl, AF.Abs),
                   waits=[(V, V.count)], inc=S)
            p.emit("scalar", lambda e: e.activation(dts[1][:], dts[0][:], AF.Exp,
                                                    scale=-1.0), waits=(), inc=S)
            sv = (S, S.count)
            p.emit("vector", lambda e: e.tensor_scalar(dts[1][:], dts[1][:], 1.0,
                                                       None, ALU.add),
                   waits=[sv], inc=V)
            vv = (V, V.count)
            p.emit("scalar", lambda e: e.activation(dts[1][:], dts[1][:], AF.Ln),
                   waits=[vv], inc=S)
            p.emit("vector", lambda e, dsl=dsl: e.tensor_relu(dts[0][:], dsl),
                   waits=[(S, S.count)], inc=V)
            p.emit("vector", lambda e: e.tensor_tensor(dtv[:], dts[0][:], dts[1][:],
                                                       ALU.add), waits=(), inc=V)
            p.emit("vector", lambda e: e.tensor_scalar(av_t[:], dtv[:], nega[:],
                                                       None, ALU.mult),
                   waits=(), inc=V)
            p.emit("vector", lambda e: e.tensor_tensor_scan(
                ac_t[:], av_t[:], zer32[:], 0.0, ALU.add, ALU.bypass),
                waits=(), inc=V)
            vv = (V, V.count)
            p.emit("scalar", lambda e: e.activation(ea0[:], ac_t[:], AF.Exp),
                   waits=[vv, (D, D.count)], inc=S)
            p.emit("vector", lambda e: e.tensor_copy(hibt[:], ac_t[:]),
                   waits=[(TR, TR.count)], inc=V)
            p.emit("vector", lambda e: e.tensor_copy(dts[0][:], hibt[:]),
                   waits=(), inc=V)
            p.emit("vector", lambda e: e.tensor_tensor(lobt[:], ac_t[:], dts[0][:],
                                                       ALU.subtract), waits=(), inc=V)
            p.emit("vector", lambda e: e.tensor_scalar(
                lhsTD[0:32, :], hibt[:], -1.0, None, ALU.mult),
                waits=[(E, E.count), casts_d], inc=V)
            p.emit("vector", lambda e: e.tensor_scalar(
                lhsTD[32:64, :], lobt[:], -1.0, None, ALU.mult), waits=(), inc=V)
            lhsTD_ready = (V, V.count)
            vv = (V, V.count)
            p.emit("scalar", lambda e: e.dma_start(out=rhsD[64:65, :], in_=hibt[:]),
                   waits=[vv, (E, E.count), casts_d], inc=TR)
            p.emit("scalar", lambda e: e.dma_start(out=rhsD[65:66, :], in_=lobt[:]),
                   waits=(), inc=TR)
            flat_t = (TR, TR.count)

            vv = (V, V.count)
            p.emit("tensor", lambda e: e.transpose(P[4][0:128, 0:32], dtv[:],
                                                   idf[0:32, 0:32]),
                   waits=[vv, (S, S.count)] + bankg[4], inc=E)
            sv = (S, S.count)
            p.emit("tensor", lambda e: e.transpose(P[4][0:128, 32:64], ea0[:],
                                                   idf[0:32, 0:32]),
                   waits=[sv], inc=E)
            p.emit("tensor", lambda e, sub=sub: e.matmul(
                P[4][0:128, 192:320], Bb[:, sub * 128:(sub + 1) * 128],
                Cb[:, sub * 128:(sub + 1) * 128], start=True, stop=True),
                waits=[cb_ready], inc=E)
            ev = (E, E.count)
            p.emit("vector", lambda e: e.tensor_copy(dtT[:], P[4][0:128, 0:32]),
                   waits=[ev, (D, D.count)], inc=V)
            p.emit("vector", lambda e: e.tensor_copy(ea0t[:], P[4][0:128, 32:64]),
                   waits=(), inc=V)

            p.emit("vector", lambda e: e.tensor_tensor(smm[:], P[4][0:128, 192:320],
                                                       utb[:], ALU.mult),
                   waits=[casts_d], inc=V)
            bankg[4] = [(V, V.count)]
            sv2 = (S, S.count)
            p.emit("gpsimd", lambda e: e.dma_start(out=dscr[0:32],
                                                   in_=ea0[:, 127:128]),
                   waits=[sv2, (V, V.count), (D, D.count)], inc=D)
            p.emit("gpsimd", lambda e: e.dma_start(
                out=datb[:], in_=dscr[0:32].unsqueeze(0).broadcast_to([128, 32])),
                waits=[(D, D.count)], inc=D)
            datb_d = (D, D.count)

            for eo in range(8):
                bank = P[5 + eo % 2]
                p.emit("tensor", lambda e, eo=eo, bank=bank: e.matmul(
                    bank[:, 0:512], lhsTD[:, :], rhsD[:, eo * 512:(eo + 1) * 512],
                    start=True, stop=True),
                    waits=([flat_t, lhsTD_ready, casts_d] if eo == 0 else [])
                    + bankg[5 + eo % 2], inc=E)
                ev = (E, E.count)
                half = emf[:, 0:512]
                p.emit("scalar", lambda e, half=half, bank=bank: e.activation(
                    half, bank[:, 0:512], AF.Relu, scale=-1.0),
                    waits=[ev], inc=S)
                bankg[5 + eo % 2] = [(S, S.count)]
                p.emit("scalar", lambda e, eo=eo, half=half: e.activation(
                    Etile[:, eo * 512:(eo + 1) * 512], half, AF.Exp, scale=-1.0),
                    waits=[(E, E.count)], inc=S)
            sv_e = (S, S.count)

            dtT_b = dtT[:].unsqueeze(2).broadcast_to([128, 32, 64])
            p.emit("vector", lambda e, b=dtT_b: e.tensor_tensor(
                X2[:].rearrange("p (h d) -> p h d", h=32), XC[:].rearrange(
                    "p (h d) -> p h d", h=32), b, ALU.mult),
                waits=[xc_t, (E, E.count)], inc=V)
            fcol = Etile[:].rearrange("p (h t) -> p h t", h=32)[:, :, 127]
            p.emit("vector", lambda e, f=fcol: e.tensor_copy(fsave[:], f),
                   waits=[sv_e], inc=V)
            smm_b = smm[:].unsqueeze(1).broadcast_to([128, 32, 128])
            p.emit("vector", lambda e, b=smm_b: e.tensor_tensor(
                Etile[:].rearrange("p (h t) -> p h t", h=32),
                Etile[:].rearrange("p (h t) -> p h t", h=32), b, ALU.mult),
                waits=(), inc=V)
            p.emit("vector", lambda e: e.tensor_tensor(XC[:], XC[:], dbb[:],
                                                       ALU.mult),
                waits=[casts_d], inc=V)
            ms_v = (V, V.count)

            for grp in range(4):
                bank = P[7 if grp % 2 == 0 else 4]

                def mm_y2(e, grp=grp, bank=bank, sub=sub):
                    last = None
                    for hh in range(8):
                        h = grp * 8 + hh
                        last = e.matmul(
                            bank[:, hh * 64:(hh + 1) * 64],
                            Cb[:, sub * 128:(sub + 1) * 128],
                            hbt[:, h * 64:(h + 1) * 64], start=True, stop=True)
                    return last
                p.emit("tensor", mm_y2,
                       waits=[cb_ready, hbt_ready] + bankg[7 if grp % 2 == 0 else 4],
                       inc=E)
                ev = (E, E.count)
                ea_b = ea0t[:, grp * 8:(grp + 1) * 8].unsqueeze(2).broadcast_to(
                    [128, 8, 64])
                p.emit("vector", lambda e, grp=grp, bank=bank, b=ea_b: e.tensor_tensor(
                    ytile[:, grp * 512:(grp + 1) * 512].rearrange(
                        "p (h d) -> p h d", h=8),
                    bank[:, 0:512].rearrange("p (h d) -> p h d", h=8), b, ALU.mult),
                    waits=[ev, (TY, TY.count)], inc=V)
                bankg[7 if grp % 2 == 0 else 4] = [(V, V.count)]
            for grp in range(4):
                bank = P[5 if grp % 2 == 0 else 6]

                def mm_y1(e, grp=grp, bank=bank):
                    last = None
                    for hh in range(8):
                        h = grp * 8 + hh
                        last = e.matmul(
                            bank[:, hh * 64:(hh + 1) * 64],
                            Etile[:, h * 128:(h + 1) * 128],
                            X2[:, h * 64:(h + 1) * 64], start=True, stop=True)
                    return last
                p.emit("tensor", mm_y1,
                       waits=[ms_v] + bankg[5 if grp % 2 == 0 else 6], inc=E)
                ev = (E, E.count)
                p.emit("vector", lambda e, grp=grp, bank=bank: e.scalar_tensor_tensor(
                    ytile[:, grp * 512:(grp + 1) * 512],
                    ytile[:, grp * 512:(grp + 1) * 512], 1.0, bank[:, 0:512],
                    ALU.mult, ALU.add), waits=[ev], inc=V)
                bankg[5 if grp % 2 == 0 else 6] = [(V, V.count)]
                p.emit("vector", lambda e, grp=grp: e.tensor_tensor(
                    ytile[:, grp * 512:(grp + 1) * 512],
                    ytile[:, grp * 512:(grp + 1) * 512],
                    XC[:, grp * 512:(grp + 1) * 512], ALU.add), waits=(), inc=V)
            vv = (V, V.count)
            p.emit("scalar", lambda e: e.dma_start_transpose(
                yT[:].rearrange("p (fc t) -> p fc t", fc=16),
                ytile[:].rearrange("t (fc f) -> t fc f", fc=16)),
                waits=[vv, (E, E.count)], inc=TY)
            yt_t = (TY, TY.count)
            sz_ap = szb[:].rearrange("p (fc t) -> p fc t", fc=16)[
                :, :, sub * 128:(sub + 1) * 128]
            p.emit("vector", lambda e, a=sz_ap: e.tensor_tensor(
                yT[:].rearrange("p (fc t) -> p fc t", fc=16),
                yT[:].rearrange("p (fc t) -> p fc t", fc=16), a, ALU.mult),
                waits=[yt_t], inc=V)
            szb_free = (V, V.count)
            vv = (V, V.count)

            def mm_oproj(e):
                last = None
                for k in range(16):
                    e.matmul(P[0][:, 0:512], yT[:, k * 128:(k + 1) * 128],
                             woTb[:, k * DM: k * DM + 512],
                             start=(k == 0), stop=(k == 15))
                    last = e.matmul(P[1][:, 0:512], yT[:, k * 128:(k + 1) * 128],
                                    woTb[:, k * DM + 512: (k + 1) * DM],
                                    start=(k == 0), stop=(k == 15))
                return last
            p.emit("tensor", mm_oproj,
                   waits=[vv] + bankg[0] + bankg[1]
                   + ([woTb_ready] if c == 0 and sub == 0 else []), inc=E)
            ev = (E, E.count)
            slot = g % 3
            t0 = t0ring[:, (g % 2) * DM:(g % 2 + 1) * DM]
            p.emit("vector", lambda e, t0=t0: e.tensor_tensor(
                P[0][:, 0:512], P[0][:, 0:512], t0[:, 0:512], ALU.add),
                waits=[ev, (S, S.count)], inc=V)
            p.emit("vector", lambda e, t0=t0: e.tensor_tensor(
                P[1][:, 0:512], P[1][:, 0:512], t0[:, 512:1024], ALU.add),
                waits=(), inc=V)
            vv = (V, V.count)
            p.emit("scalar", lambda e, g=g: e.activation(
                xnscr[:, 0:512], P[0][:, 0:512], AF.Square,
                accum_out=asq[:, g:g + 1]), waits=[vv], inc=S)
            p.emit("scalar", lambda e, g=g: e.activation(
                xnscr[:, 512:1024], P[1][:, 0:512], AF.Square,
                accum_out=tscr[:, g:g + 1]), waits=(), inc=S)
            p.emit("vector", lambda e, g=g: e.tensor_reduce(
                asum[:, g:g + 1], P[0][:, 0:512], AX.X, ALU.add),
                waits=[(S, S.count)], inc=V)
            p.emit("vector", lambda e, g=g: e.tensor_reduce(
                avar[:, g:g + 1], P[1][:, 0:512], AX.X, ALU.add), waits=(), inc=V)
            p.emit("vector", lambda e, g=g: e.tensor_tensor(
                asum[:, g:g + 1], asum[:, g:g + 1], avar[:, g:g + 1], ALU.add),
                waits=(), inc=V)
            sv = (S, S.count)
            p.emit("vector", lambda e, g=g: e.tensor_tensor(
                asq[:, g:g + 1], asq[:, g:g + 1], tscr[:, g:g + 1], ALU.add),
                waits=[sv], inc=V)
            p.emit("vector", lambda e, g=g: e.tensor_scalar(
                amu[:, g:g + 1], asum[:, g:g + 1], 1.0 / DM, None, ALU.mult),
                waits=(), inc=V)
            p.emit("vector", lambda e, g=g: e.tensor_tensor(
                tscr[:, g:g + 1], amu[:, g:g + 1], amu[:, g:g + 1], ALU.mult),
                waits=(), inc=V)
            p.emit("vector", lambda e, g=g: e.tensor_scalar(
                avar[:, g:g + 1], asq[:, g:g + 1], 1.0 / DM, None, ALU.mult),
                waits=(), inc=V)
            p.emit("vector", lambda e, g=g: e.tensor_tensor(
                avar[:, g:g + 1], avar[:, g:g + 1], tscr[:, g:g + 1], ALU.subtract),
                waits=(), inc=V)
            vv = (V, V.count)
            emit_rstd([vv], avar[:, g:g + 1], arstd[:, g:g + 1], EPS_LN)
            sv = (S, S.count)
            xs = xring[:, slot * DM:(slot + 1) * DM]
            p.emit("vector", lambda e, g=g: e.tensor_scalar(
                P[0][:, 0:512], P[0][:, 0:512], amu[:, g:g + 1], arstd[:, g:g + 1],
                ALU.subtract, ALU.mult), waits=[sv], inc=V)
            p.emit("vector", lambda e, g=g: e.tensor_scalar(
                P[1][:, 0:512], P[1][:, 0:512], amu[:, g:g + 1], arstd[:, g:g + 1],
                ALU.subtract, ALU.mult), waits=(), inc=V)
            p.emit("vector", lambda e, xs=xs: e.tensor_tensor(
                xs[:, 0:512], xs[:, 0:512], P[0][:, 0:512], ALU.add),
                waits=(), inc=V)
            p.emit("vector", lambda e, xs=xs: e.tensor_tensor(
                xs[:, 512:1024], xs[:, 512:1024], P[1][:, 0:512], ALU.add),
                waits=(), inc=V)
            bankg[0] = [(V, V.count)]
            bankg[1] = [(V, V.count)]
            vv = (V, V.count)
            p.emit("gpsimd", lambda e, g=g, xs=xs: e.dma_start(
                out=x1s[g * 128:(g + 1) * 128, :], in_=xs), waits=[vv], inc=D)
            if sub == 0 and c + 1 < NT1:
                gn = g0 + 3
                nslot = gn % 3
                p.emit("gpsimd", lambda e, gn=gn, nslot=nslot: e.dma_start(
                    out=xring[:, nslot * DM:(nslot + 1) * DM],
                    in_=x_in[gn * 128:(gn + 1) * 128, :]),
                    waits=[(D, D.count)], inc=D)

            fb = fsave[:].unsqueeze(2).broadcast_to([128, 32, 64])
            p.emit("vector", lambda e, fb=fb: e.tensor_tensor(
                X2[:].rearrange("p (h d) -> p h d", h=32),
                X2[:].rearrange("p (h d) -> p h d", h=32), fb, ALU.mult),
                waits=[(E, E.count)], inc=V)
            for grp in range(4):
                bank = P[5 if grp % 2 == 0 else 6]

                def mm_state(e, grp=grp, bank=bank, sub=sub):
                    last = None
                    for hh in range(8):
                        h = grp * 8 + hh
                        last = e.matmul(
                            bank[:, hh * 64:(hh + 1) * 64],
                            btm2[:, sub * 128:(sub + 1) * 128],
                            X2[:, h * 64:(h + 1) * 64], start=True, stop=True)
                    return last
                p.emit("tensor", mm_state,
                       waits=[(V, V.count), btm_t]
                       + bankg[5 if grp % 2 == 0 else 6], inc=E)
                ev = (E, E.count)
                da_b = datb[:, grp * 8:(grp + 1) * 8].unsqueeze(2).broadcast_to(
                    [128, 8, 64])
                hsl = htile[:, grp * 512:(grp + 1) * 512]
                p.emit("vector", lambda e, hsl=hsl, b=da_b: e.tensor_tensor(
                    hsl.rearrange("p (h d) -> p h d", h=8),
                    hsl.rearrange("p (h d) -> p h d", h=8), b, ALU.mult),
                    waits=[datb_d, ev], inc=V)
                p.emit("vector", lambda e, hsl=hsl, bank=bank: e.tensor_tensor(
                    hsl, hsl, bank[:, 0:512], ALU.add), waits=(), inc=V)
                bankg[5 if grp % 2 == 0 else 6] = [(V, V.count)]
                vv = (V, V.count)
                p.emit("scalar", lambda e, grp=grp, hsl=hsl: e.activation(
                    hbt[:, grp * 512:(grp + 1) * 512], hsl, AF.Copy),
                    waits=[vv], inc=S)
            hbt_ready = (S, S.count)
            prev_state_e = [(E, E.count)]
            run_tasks(8)

        if c + 1 < n_p1_chunks:
            for g in (g0 + 2, g0 + 3):
                p1_stats(g)
    run_tasks(len(quant_tasks))

    if not do_p2:
        p1stack.close()
        _finalize(p, nc, stack, dram, dumps, locals())
        return nc

    p1stack.close()
    p.fence(ALL)
    wgq = sb("wgq", [128, 8 * INNER], FP8)
    wuq = sb("wuq", [128, 8 * INNER], FP8)
    wdq = sb("wdq", [128, 32 * DM], FP8)
    x1ring = sb("x1ring", [128, 3 * DM], F32)
    xn1 = sb("xn1", [128, DM], F32)
    xqf = sb("xqf", [128, DM], F32)
    xqb = sb("xqb", [128, DM], BF16)
    xqT = sb("xqT", [128, DM], BF16)
    ga = sb("ga", [128, INNER], BF16)
    ua = sb("ua", [128, INNER], BF16)
    ttile = sb("ttile", [128, INNER], BF16)
    tqf = sb("tqf", [128, INNER], F32)
    tqb = sb("tqb", [128, INNER], BF16)
    tqT = sb("tqT", [128, INNER], BF16)
    dfull = sb("dfull", [128, 2 * DM], F32)
    mu2 = tiny("mu2"); var2 = tiny("var2"); rstd2c = tiny("rstd2c")
    sum2 = tiny("sum2"); sq2 = tiny("sq2")
    amax1 = tiny("amax1"); sx1 = tiny("sx1"); vgc = tiny("vgc"); vuc = tiny("vuc")
    amax2 = tiny("amax2"); sx2 = tiny("sx2"); vdc = tiny("vdc")
    mud = tiny("mud"); vard = tiny("vard"); rstdd = tiny("rstdd")
    sumd = tiny("sumd"); sqd = tiny("sqd")
    t2scr = tiny("t2scr")

    barrier = p.barrier_waits(ALL)
    p.emit("sync", lambda e: e.dma_start(
        out=wgq[:].rearrange("p (k c) -> p k c", k=8),
        in_=wq_g[:].rearrange("(k p) c -> p k c", p=128)), waits=barrier, inc=W)
    p.emit("scalar", lambda e: e.dma_start(
        out=wuq[:].rearrange("p (k c) -> p k c", k=8),
        in_=wq_u[:].rearrange("(k p) c -> p k c", p=128)), waits=barrier, inc=U2)
    p.emit("gpsimd", lambda e: e.dma_start(
        out=wdq[:].rearrange("p (k c) -> p k c", k=32),
        in_=wq_d[:].rearrange("(k p) c -> p k c", p=128)), waits=barrier, inc=U3)
    wq_w = [(W, W.count), (U2, U2.count)]
    wd_w = [(U3, U3.count)]

    def p2_stats(cc):
        slot = cc % 3
        xs = x1ring[:, slot * DM:(slot + 1) * DM]
        p.emit("scalar", lambda e, xs=xs, cc=cc: e.activation(
            xqb[:], xs, AF.Square, accum_out=sq2[:, cc % 16:cc % 16 + 1]),
            waits=[(D, D.count), (TX, TX.count)], inc=S)
        p.emit("vector", lambda e, xs=xs, cc=cc: e.tensor_reduce(
            sum2[:, cc % 16:cc % 16 + 1], xs, AX.X, ALU.add),
            waits=[(D, D.count)], inc=V)
        i = cc % 16
        p.emit("vector", lambda e, i=i: e.tensor_scalar(
            mu2[:, i:i + 1], sum2[:, i:i + 1], 1.0 / DM, None, ALU.mult),
            waits=(), inc=V)
        p.emit("vector", lambda e, i=i: e.tensor_tensor(
            t2scr[:, i:i + 1], mu2[:, i:i + 1], mu2[:, i:i + 1], ALU.mult),
            waits=(), inc=V)
        sv = (S, S.count)
        p.emit("vector", lambda e, i=i: e.tensor_scalar(
            var2[:, i:i + 1], sq2[:, i:i + 1], 1.0 / DM, None, ALU.mult),
            waits=[sv], inc=V)
        p.emit("vector", lambda e, i=i: e.tensor_tensor(
            var2[:, i:i + 1], var2[:, i:i + 1], t2scr[:, i:i + 1], ALU.subtract),
            waits=(), inc=V)
        vv = (V, V.count)
        emit_rstd([vv], var2[:, i:i + 1], rstd2c[:, i:i + 1], EPS_LN)

    p.emit("gpsimd", lambda e: e.dma_start(
        out=x1ring[:, 0:DM], in_=x1s[0:128, :]), waits=barrier, inc=D)
    p2_stats(0)

    def p2_finalize(cc):
        i = cc % 16
        dsl = dfull[:, (cc % 2) * DM:(cc % 2 + 1) * DM]
        slot = cc % 3
        xs = x1ring[:, slot * DM:(slot + 1) * DM]
        sv = (S, S.count)
        p.emit("vector", lambda e, dsl=dsl, i=i: e.tensor_scalar(
            xn1[:], dsl, mud[:, i:i + 1], rstdd[:, i:i + 1],
            ALU.subtract, ALU.mult), waits=[sv, (D, D.count)], inc=V)
        p.emit("vector", lambda e, xs=xs: e.tensor_tensor(
            xn1[:], xn1[:], xs, ALU.add), waits=(), inc=V)
        vv = (V, V.count)
        p.emit("gpsimd", lambda e, cc=cc: e.dma_start(
            out=out[cc * 128:(cc + 1) * 128, :], in_=xn1[:]), waits=[vv], inc=D)

    p2bank = []
    gu_done = []
    dn_done = []
    for cc in range(NT2):
        i = cc % 16
        slot = cc % 3
        xs = x1ring[:, slot * DM:(slot + 1) * DM]
        if cc + 1 < NT2:
            nslot = (cc + 1) % 3
            p.emit("gpsimd", lambda e, cc=cc, nslot=nslot: e.dma_start(
                out=x1ring[:, nslot * DM:(nslot + 1) * DM],
                in_=x1s[(cc + 1) * 128:(cc + 2) * 128, :]),
                waits=[(V, V.count)], inc=D)
            p2_stats(cc + 1)
        if cc > 0:
            ip = (cc - 1) % 16
            sv = (S, S.count)
            p.emit("vector", lambda e, ip=ip: e.tensor_scalar(
                mud[:, ip:ip + 1], sumd[:, ip:ip + 1], 1.0 / DM, None, ALU.mult),
                waits=[sv], inc=V)
            p.emit("vector", lambda e, ip=ip: e.tensor_tensor(
                t2scr[:, ip:ip + 1], mud[:, ip:ip + 1], mud[:, ip:ip + 1], ALU.mult),
                waits=(), inc=V)
            p.emit("vector", lambda e, ip=ip: e.tensor_scalar(
                vard[:, ip:ip + 1], sqd[:, ip:ip + 1], 1.0 / DM, None, ALU.mult),
                waits=(), inc=V)
            p.emit("vector", lambda e, ip=ip: e.tensor_tensor(
                vard[:, ip:ip + 1], vard[:, ip:ip + 1], t2scr[:, ip:ip + 1],
                ALU.subtract), waits=(), inc=V)
            vv = (V, V.count)
            emit_rstd([vv], vard[:, ip:ip + 1], rstdd[:, ip:ip + 1], EPS_LN)
            p2_finalize(cc - 1)

        sv = (S, S.count)
        p.emit("vector", lambda e, xs=xs, i=i: e.tensor_scalar(
            xn1[:], xs, mu2[:, i:i + 1], rstd2c[:, i:i + 1],
            ALU.subtract, ALU.mult), waits=[sv, (D, D.count)], inc=V)
        p.emit("vector", lambda e, i=i: e.tensor_reduce(
            amax1[:, i:i + 1], xn1[:], AX.X, ALU.max, apply_absolute_value=True),
            waits=(), inc=V)
        p.emit("vector", lambda e, i=i: e.tensor_scalar(
            amax1[:, i:i + 1], amax1[:, i:i + 1], 1e-5, None, ALU.max),
            waits=(), inc=V)
        p.emit("vector", lambda e, i=i: e.reciprocal(
            sx1[:, i:i + 1], amax1[:, i:i + 1]), waits=(), inc=V)
        p.emit("vector", lambda e, i=i: e.tensor_scalar(
            sx1[:, i:i + 1], sx1[:, i:i + 1], 127.0, None, ALU.mult),
            waits=(), inc=V)
        p.emit("vector", lambda e, i=i: e.tensor_scalar(
            t2scr[:, i:i + 1], amax1[:, i:i + 1], 1.0 / 127.0, None, ALU.mult),
            waits=(), inc=V)
        p.emit("vector", lambda e, i=i: e.tensor_tensor(
            vgc[:, i:i + 1], t2scr[:, i:i + 1], mabs[:, 0:1], ALU.mult),
            waits=(), inc=V)
        p.emit("vector", lambda e, i=i: e.tensor_tensor(
            vuc[:, i:i + 1], t2scr[:, i:i + 1], mabs[:, 1:2], ALU.mult),
            waits=(), inc=V)
        p.emit("vector", lambda e, i=i: e.tensor_scalar(
            xqf[:], xn1[:], sx1[:, i:i + 1], MAGIC, ALU.mult, ALU.add),
            waits=(), inc=V)
        p.emit("vector", lambda e: e.tensor_scalar(
            xqb[:], xqf[:], MAGIC, None, ALU.subtract),
               waits=[(S, S.count), (TX, TX.count)], inc=V)
        vv = (V, V.count)
        p.emit("scalar", lambda e: e.dma_start_transpose(
            xqT[:].rearrange("p (k t) -> p k t", k=8),
            xqb[:].rearrange("t (k f) -> t k f", k=8)),
            waits=[vv] + gu_done, inc=TX)
        tq_t = (TX, TX.count)

        for half in range(2):
            def mm_gu(e, half=half):
                last = None
                for k in range(8):
                    lhs = xqT[:, k * 128:(k + 1) * 128]
                    for s_ in range(4):
                        col = half * 2048 + s_ * 512
                        e.matmul(P[s_][:, 0:512], lhs,
                                 wgq[:, k * INNER + col: k * INNER + col + 512],
                                 start=(k == 0), stop=(k == 7))
                        last = e.matmul(P[4 + s_][:, 0:512], lhs,
                                        wuq[:, k * INNER + col: k * INNER + col + 512],
                                        start=(k == 0), stop=(k == 7))
                return last
            p.emit("tensor", mm_gu,
                   waits=[tq_t, (S, S.count)] + wq_w + p2bank, inc=E)
            gu_done = [(E, E.count)]
            ev = (E, E.count)
            for s_ in range(4):
                col = half * 2048 + s_ * 512
                p.emit("scalar", lambda e, s_=s_, col=col, i=i: e.activation(
                    ga[:, col:col + 512], P[s_][:, 0:512], AF.Silu,
                    scale=vgc[:, i:i + 1]),
                    waits=[ev, (V, V.count)] if s_ == 0 else (), inc=S)
            for s_ in range(4):
                col = half * 2048 + s_ * 512
                p.emit("scalar", lambda e, s_=s_, col=col, i=i: e.activation(
                    ua[:, col:col + 512], P[4 + s_][:, 0:512], AF.Copy,
                    scale=vuc[:, i:i + 1]), waits=(), inc=S)
        sv = (S, S.count)
        p.emit("vector", lambda e: e.tensor_tensor(ttile[:], ga[:], ua[:], ALU.mult),
               waits=[sv], inc=V)
        p.emit("vector", lambda e, i=i: e.tensor_reduce(
            amax2[:, i:i + 1], ttile[:], AX.X, ALU.max, apply_absolute_value=True),
            waits=(), inc=V)
        p.emit("vector", lambda e, i=i: e.tensor_scalar(
            amax2[:, i:i + 1], amax2[:, i:i + 1], 1e-5, None, ALU.max),
            waits=(), inc=V)
        p.emit("vector", lambda e, i=i: e.reciprocal(
            sx2[:, i:i + 1], amax2[:, i:i + 1]), waits=(), inc=V)
        p.emit("vector", lambda e, i=i: e.tensor_scalar(
            sx2[:, i:i + 1], sx2[:, i:i + 1], 127.0, None, ALU.mult),
            waits=(), inc=V)
        p.emit("vector", lambda e, i=i: e.tensor_scalar(
            t2scr[:, i:i + 1], amax2[:, i:i + 1], 1.0 / 127.0, None, ALU.mult),
            waits=(), inc=V)
        p.emit("vector", lambda e, i=i: e.tensor_tensor(
            vdc[:, i:i + 1], t2scr[:, i:i + 1], mabs[:, 2:3], ALU.mult),
            waits=(), inc=V)
        p.emit("vector", lambda e, i=i: e.tensor_scalar(
            tqf[:], ttile[:], sx2[:, i:i + 1], MAGIC, ALU.mult, ALU.add),
            waits=(), inc=V)
        p.emit("vector", lambda e: e.tensor_scalar(
            tqb[:], tqf[:], MAGIC, None, ALU.subtract),
            waits=[(TY, TY.count)], inc=V)
        vv = (V, V.count)
        p.emit("scalar", lambda e: e.dma_start_transpose(
            tqT[:].rearrange("p (k t) -> p k t", k=32),
            tqb[:].rearrange("t (k f) -> t k f", k=32)),
            waits=[vv] + dn_done, inc=TY)
        tq2_t = (TY, TY.count)

        def mm_down(e):
            last = None
            for k in range(32):
                lhs = tqT[:, k * 128:(k + 1) * 128]
                e.matmul(P[6][:, 0:512], lhs, wdq[:, k * DM: k * DM + 512],
                         start=(k == 0), stop=(k == 31))
                last = e.matmul(P[7][:, 0:512], lhs,
                                wdq[:, k * DM + 512:(k + 1) * DM],
                                start=(k == 0), stop=(k == 31))
            return last
        p.emit("tensor", mm_down,
               waits=[tq2_t, (S, S.count)] + wd_w, inc=E)
        dn_done = [(E, E.count)]
        ev = (E, E.count)
        dsl = dfull[:, (cc % 2) * DM:(cc % 2 + 1) * DM]
        p.emit("vector", lambda e, dsl=dsl, i=i: e.tensor_scalar(
            dsl[:, 0:512], P[6][:, 0:512], vdc[:, i:i + 1], None, ALU.mult),
            waits=[ev], inc=V)
        p.emit("vector", lambda e, dsl=dsl, i=i: e.tensor_scalar(
            dsl[:, 512:1024], P[7][:, 0:512], vdc[:, i:i + 1], None, ALU.mult),
            waits=(), inc=V)
        p2bank = [(V, V.count)]
        vv = (V, V.count)
        p.emit("scalar", lambda e, dsl=dsl, i=i: e.activation(
            xqb[:], dsl, AF.Square, accum_out=sqd[:, i:i + 1]),
            waits=[vv, (TX, TX.count)], inc=S)
        p.emit("vector", lambda e, dsl=dsl, i=i: e.tensor_reduce(
            sumd[:, i:i + 1], dsl, AX.X, ALU.add), waits=(), inc=V)

    ip = (NT2 - 1) % 16
    sv = (S, S.count)
    p.emit("vector", lambda e, ip=ip: e.tensor_scalar(
        mud[:, ip:ip + 1], sumd[:, ip:ip + 1], 1.0 / DM, None, ALU.mult),
        waits=[sv], inc=V)
    p.emit("vector", lambda e, ip=ip: e.tensor_tensor(
        t2scr[:, ip:ip + 1], mud[:, ip:ip + 1], mud[:, ip:ip + 1], ALU.mult),
        waits=(), inc=V)
    p.emit("vector", lambda e, ip=ip: e.tensor_scalar(
        vard[:, ip:ip + 1], sqd[:, ip:ip + 1], 1.0 / DM, None, ALU.mult),
        waits=(), inc=V)
    p.emit("vector", lambda e, ip=ip: e.tensor_tensor(
        vard[:, ip:ip + 1], vard[:, ip:ip + 1], t2scr[:, ip:ip + 1], ALU.subtract),
        waits=(), inc=V)
    vv = (V, V.count)
    emit_rstd([vv], vard[:, ip:ip + 1], rstdd[:, ip:ip + 1], EPS_LN)
    p2_finalize(NT2 - 1)

    _finalize(p, nc, stack, dram, dumps, locals())
    return nc


def _finalize(p, nc, stack, dram, dumps, env):
    Dch = env["D"]
    allch = env["ALL"]
    for name in dumps:
        tile = env[name]
        shape = list(tile[:].shape)
        dd = nc.dram_tensor("dbg_" + name, shape, tile[:].dtype,
                            kind="ExternalOutput").ap()
        p.emit("gpsimd", lambda e, t=tile, d=dd: e.dma_start(out=d, in_=t[:]),
               waits=p.barrier_waits(allch), inc=Dch)
    p.emit("gpsimd", lambda e: e.wait_ge(Dch.sem, Dch.count), waits=(), inc=None)
    with stack:
        block = stack.enter_context(nc.Block())
        p.replay(block, None)


_CACHE = {}


def _host_consts():
    hm = np.zeros((96, INNER), np.float32)
    for h in range(32):
        hm[h, h * 128:(h + 1) * 128] = 1.0
        hm[32 + h, h * 128:(h + 1) * 128] = 1.0
    lhsd = np.zeros((96, 128), np.float32)
    lhsd[64, :] = 1.0
    lhsd[65, :] = 1.0
    cid = np.eye(32, dtype=np.float32)
    ut = np.triu(np.ones((128, 128), np.float32))
    return hm, lhsd, cid, ut


def kernel(L0_attn_x, L0_attn_norm_w, L0_attn_in_proj_w, L0_attn_conv_w,
           L0_attn_conv_b, L0_attn_A_log, L0_attn_D, L0_attn_dt_bias,
           L0_attn_out_proj_w, L0_ffn_w_gate, L0_ffn_w_up, L0_ffn_w_down):
    x = np.ascontiguousarray(np.asarray(L0_attn_x, dtype=np.float32))
    wiT = np.ascontiguousarray(np.asarray(L0_attn_in_proj_w, np.float32).T)
    woT = np.ascontiguousarray(np.asarray(L0_attn_out_proj_w, np.float32).T)
    wgT = np.ascontiguousarray(np.asarray(L0_ffn_w_gate, np.float32).T)
    wuT = np.ascontiguousarray(np.asarray(L0_ffn_w_up, np.float32).T)
    wdT = np.ascontiguousarray(np.asarray(L0_ffn_w_down, np.float32).T)
    normw = np.ascontiguousarray(
        np.asarray(L0_attn_norm_w, np.float32).reshape(8, 128).T)
    convw = np.ascontiguousarray(np.asarray(L0_attn_conv_w, np.float32)
                                 .reshape(16, 128, KC).transpose(1, 0, 2)
                                 .reshape(128, 16 * KC))
    convb = np.ascontiguousarray(
        np.asarray(L0_attn_conv_b, np.float32).reshape(16, 128).T)
    alog = np.ascontiguousarray(np.asarray(L0_attn_A_log, np.float32)
                                .reshape(32, 1))
    dtb = np.ascontiguousarray(np.asarray(L0_attn_dt_bias, np.float32)
                               .reshape(32, 1))
    db = np.ascontiguousarray(np.broadcast_to(
        np.repeat(np.asarray(L0_attn_D, np.float32), HD)[None, :], (128, DI)))
    hm, lhsd, cid, ut = _host_consts()

    if "nc" not in _CACHE:
        _CACHE["nc"] = build_program()
    nc = _CACHE["nc"]

    base = dict(wiT=wiT, woT=woT, wgT=wgT, wuT=wuT, wdT=wdT, normw=normw,
                convw=convw, convb=convb, alog=alog, dtb=dtb, db=db,
                c_id=cid, c_ut=ut, c_rhsd=hm, c_lhsd=lhsd)
    in_maps = []
    for c in range(8):
        m = dict(base)
        m["x_in"] = np.ascontiguousarray(x[c])
        in_maps.append(m)
    res = run_bass_kernel_spmd(nc, in_maps, core_ids=list(range(8)))
    return np.stack([np.asarray(r["out"]) for r in res.results], axis=0)
